# version 03 (final)
# baseline (speedup 1.0000x reference)
import numpy as np
import ml_dtypes
from contextlib import ExitStack

import concourse.bass as bass
import concourse.mybir as mybir
from concourse.bass_utils import run_bass_kernel_spmd

F32 = mybir.dt.float32
BF16 = mybir.dt.bfloat16
AF = mybir.ActivationFunctionType
ALU = mybir.AluOpType
AX = mybir.AxisListType
BF = ml_dtypes.bfloat16

C = 1024
NCH = 8
T_FULL = 4096
B = 4
N_CORES = 8

ENGINES = ("sync", "gpsimd", "vector", "scalar", "tensor")


class Sched:
    def __init__(self, nc, ctx):
        self.nc = nc
        self.ctx = ctx
        self.q = {e: [] for e in ENGINES}
        self.sems = {}
        self.cnt = {}
        self.lastwait = {}

    def sem(self, name):
        if name not in self.sems:
            self.sems[name] = self.ctx.enter_context(self.nc.semaphore(name))
            self.cnt[name] = 0
        return self.sems[name]

    def emit(self, eng, fn):
        self.q[eng].append(fn)

    def wait(self, eng, tok):
        if tok is None:
            return
        semname, val = tok
        if val <= 0:
            return
        key = (eng, semname)
        if self.lastwait.get(key, 0) >= val:
            return
        self.lastwait[key] = val
        h = self.sem(semname)
        self.emit(eng, lambda e, h=h, v=val: e.wait_ge(h, v))

    ENG_SEM = {"tensor": "pt", "vector": "pv", "scalar": "ps",
               "gpsimd": "pg", "sync": "py"}
    SELF_WAIT = ("vector", "scalar", "gpsimd")

    def op(self, eng, fn, deps=(), inc=None):
        name = self.ENG_SEM[eng]
        if eng in self.SELF_WAIT and self.cnt.get(name, 0) > 0:
            self.wait(eng, (name, self.cnt[name]))
        for d in deps:
            self.wait(eng, d)
        h = self.sem(name)
        self.cnt[name] += 1
        v = self.cnt[name]

        def run(e, fn=fn, h=h):
            fn(e).then_inc(h, 1)
        self.emit(eng, run)
        return (name, v)

    def dma(self, eng, out, in_, deps=()):
        name = "dq_" + eng
        if self.cnt.get(name, 0) > 0:
            self.wait(eng, (name, self.cnt[name]))
        for d in deps:
            self.wait(eng, d)
        h = self.sem(name)
        self.cnt[name] += 16
        v = self.cnt[name]
        self.emit(eng, lambda e, out=out, in_=in_, h=h:
                  e.dma_start(out=out, in_=in_).then_inc(h, 16))
        return (name, v)

    def barrier(self):
        toks = [(n, self.cnt[n]) for n in list(self.cnt)
                if (n.startswith("dq_") or n in ("pt", "pv", "ps", "pg"))
                and self.cnt[n] > 0]
        for e in ENGINES:
            for tok in toks:
                self.wait(e, tok)

    def finalize(self):
        block = self.ctx.enter_context(self.nc.Block())
        for name in ENGINES:
            ops = self.q[name]

            def make(ops):
                def body(e):
                    for fn in ops:
                        fn(e)
                return body
            getattr(block, name)(make(ops))


class Rot:

    def __init__(self, banks):
        self.banks = banks
        self.toks = [None] * len(banks)
        self.i = -1

    def acquire(self):
        self.i = (self.i + 1) % len(self.banks)
        return self.banks[self.i], self.toks[self.i]

    def release(self, tok):
        self.toks[self.i] = tok


def f32in(nc, name, shape):
    return nc.dram_tensor(name, shape, F32, kind="ExternalInput").ap()


def bf16in(nc, name, shape):
    return nc.dram_tensor(name, shape, BF16, kind="ExternalInput").ap()


def outap(nc, name, shape, dt=F32):
    return nc.dram_tensor(name, shape, dt, kind="ExternalOutput").ap()



def load_cast_weight(s, w_dram, wsb, stage, ncols, nchunks, stride=None):
    if stride is None:
        stride = ncols
    prev_cast = [None, None]
    prev_tl = None
    for c2 in range(nchunks):
        half = c2 % 2
        src = stage[:, half * ncols:(half + 1) * ncols]
        deps = [prev_cast[half]] if prev_cast[half] else []
        if prev_tl:
            deps.append(prev_tl)
        tl = s.dma("sync", src, w_dram[c2 * 128:(c2 + 1) * 128, :], deps=deps)
        prev_tl = tl
        tc = s.op("gpsimd", lambda e, c2=c2, src=src: e.tensor_copy(
            out=wsb[:, c2 * stride:c2 * stride + ncols], in_=src),
            deps=[tl], inc="pg")
        prev_cast[half] = tc
    s.barrier()
    return prev_cast[(nchunks - 1) % 2]


def ln_stats(s, x_bf, sq_bf, ones1_b, ps_mean, ps_sq, row_m, row_var, TT,
             dep_tok, nch=NCH, inv_n=None, stride=None, off=0):
    if inv_n is None:
        inv_n = 1.0 / (nch * 128)
    if stride is None:
        stride = TT
    SL = lambda c2: slice(c2 * stride + off, c2 * stride + off + TT)
    inplace = sq_bf is x_bf
    tm = None
    if inplace:
        for c2 in range(nch):
            tm = s.op("tensor", lambda e, c2=c2: e.matmul(
                ps_mean, ones1_b[:, 0:1], x_bf[:, SL(c2)],
                start=(c2 == 0), stop=(c2 == nch - 1)),
                deps=[dep_tok] if c2 == 0 else ())
    tsq = None
    for c2 in range(nch):
        tsq = s.op("scalar", lambda e, c2=c2: e.activation(
            out=sq_bf[:, SL(c2)], in_=x_bf[:, SL(c2)], func=AF.Square),
            deps=([dep_tok] if c2 == 0 else []) + ([tm] if inplace and c2 == 0 else []))
    if not inplace:
        for c2 in range(nch):
            tm = s.op("tensor", lambda e, c2=c2: e.matmul(
                ps_mean, ones1_b[:, 0:1], x_bf[:, SL(c2)],
                start=(c2 == 0), stop=(c2 == nch - 1)),
                deps=[dep_tok] if c2 == 0 else ())
    ts2 = None
    for c2 in range(nch):
        ts2 = s.op("tensor", lambda e, c2=c2: e.matmul(
            ps_sq, ones1_b[:, 0:1], sq_bf[:, SL(c2)],
            start=(c2 == 0), stop=(c2 == nch - 1)),
            deps=[tsq] if c2 == 0 else ())
    s.op("vector", lambda e: e.tensor_scalar(
        out=row_m, in0=ps_mean, scalar1=inv_n, scalar2=None, op0=ALU.mult),
        deps=[tm, ts2])
    s.op("vector", lambda e: e.tensor_tensor(
        out=row_var, in0=row_m, in1=row_m, op=ALU.mult))
    tv = s.op("vector", lambda e: e.scalar_tensor_tensor(
        out=row_var, in0=ps_sq, scalar=inv_n, in1=row_var,
        op0=ALU.mult, op1=ALU.subtract), inc="pv")
    return tv


def rstd_row(s, row_var, row_out, dep, eps_ap):
    tsq = s.op("scalar", lambda e: e.activation(
        out=row_out, in_=row_var, func=AF.Sqrt, bias=eps_ap, scale=1.0),
        deps=[dep], inc="ps")
    return s.op("vector", lambda e: e.reciprocal(out=row_out, in_=row_out),
                deps=[tsq], inc="pv")


def bcast(s, ps_dst, row, onesr, deps):
    return s.op("tensor", lambda e: e.matmul(
        ps_dst, onesr[0:1, 0:128], row, start=True, stop=True),
        deps=deps, inc="pt")


def mm_group(s, ps_ap, pairs, war_tok=None, extra_dep=None):
    n = len(pairs)
    tok = None
    for i, (lh, rh) in enumerate(pairs):
        deps = []
        if i == 0:
            if war_tok:
                deps.append(war_tok)
            if extra_dep:
                deps.append(extra_dep)
        tok = s.op("tensor", lambda e, lh=lh, rh=rh, i=i: e.matmul(
            ps_ap, lh, rh, start=(i == 0), stop=(i == n - 1)),
            deps=deps, inc="pt" if i == n - 1 else None)
    return tok



def build_l0(TT=256, dbg=False):
    NT = T_FULL // TT
    nc = bass.Bass("TRN2", target_bir_lowering=False)
    x_d = f32in(nc, "x", [C, T_FULL])
    wq_d = f32in(nc, "wqT", [C, C])
    wk_d = f32in(nc, "wkT", [C, C])
    wv_d = f32in(nc, "wvT", [C, C])
    wg_d = f32in(nc, "wgT", [C, C])
    wo_d = f32in(nc, "woT", [C, C])
    w0_d = f32in(nc, "w0", [C, C])
    a_d = f32in(nc, "ainit", [C, 8])
    b_d = f32in(nc, "binit", [8, C])
    lng_d = f32in(nc, "lng", [C, 1])
    lnb_d = f32in(nc, "lnb", [C, 1])
    loglr_d = f32in(nc, "loglr", [C, 1])
    lsg_d = f32in(nc, "lsg", [C, 1])
    posb_d = f32in(nc, "posb", [128, T_FULL])
    out_d = outap(nc, "xout", [C, T_FULL])
    dbg_d = {}
    if dbg:
        for nm in ("xin", "h", "q", "k", "v", "y0", "gate", "grad", "gs",
                   "outm", "rowm", "rowr", "rowrho"):
            shp = [1, 256] if nm.startswith("row") else [128, NCH * 256]
            dbg_d[nm] = outap(nc, "dbg_" + nm, shp)

    with ExitStack() as ctx:
        sb = lambda name, shape, dt: ctx.enter_context(nc.sbuf_tensor(name, shape, dt))
        pt_ = lambda name, shape, dt: ctx.enter_context(nc.psum_tensor(name, shape, dt))
        s = Sched(nc, ctx)
        for nm in ("pt", "pv", "ps", "pg"):
            s.sem(nm)

        wq = sb("wq", [128, NCH * C], BF16)
        wk = sb("wk", [128, NCH * C], BF16)
        wv = sb("wv", [128, NCH * C], BF16)
        wg = sb("wg", [128, NCH * C], BF16)
        wo = sb("wo", [128, NCH * C], BF16)
        w0 = sb("w0s", [128, NCH * C], BF16)
        a_sb = sb("a_sb", [128, NCH * 8], BF16)
        bin_sb = sb("bin_sb", [8, C], BF16)
        vec_sb = sb("vec_sb", [128, 32], F32)
        lnb_bf = sb("lnb_bf", [128, NCH], BF16)
        lng_col = sb("lng_col", [128, NCH], F32)
        biasqkv = sb("biasqkv", [128, 24], F32)
        ones1_b = sb("ones1_b", [128, 4], BF16)
        onesr = sb("onesr", [1, 128], F32)
        rows = sb("rows", [1, 8 * TT], F32)
        epsr = sb("epsr", [1, 2], F32)
        posb_t = sb("posb_t", [128, TT], F32)

        xt = sb("xt", [128, NCH * TT], F32)
        x_bf = sb("x_bfT", [128, NCH * TT], BF16)
        sq_b = sb("sq_bT", [128, NCH * TT], BF16)
        xmm = sb("xmm", [128, NCH * TT], F32)
        xin_b = sb("xin_b", [128, NCH * TT], BF16)
        h_b = sb("h_b", [128, NCH * TT], BF16)
        q_f = sb("q_f", [128, NCH * TT], F32)
        k_b = sb("k_bT", [128, NCH * TT], BF16)
        k_f = sb("k_f", [128, NCH * TT], F32)
        v_f = sb("v_f", [128, NCH * TT], F32)
        y0_f = sb("y0_f", [128, NCH * TT], F32)
        gate_f = sb("gate_f", [128, NCH * TT], BF16)
        grad_f = sb("grad_f", [128, NCH * TT], F32)
        gs_f = sb("gs_f", [128, NCH * TT], F32)
        outm_b = sb("outm_b", [128, NCH * TT], BF16)
        ka_b = sb("ka_b", [8, TT], BF16)
        carry = sb("carry", [128, NCH], F32)
        ones_f = sb("ones_f", [128, TT], F32)

        p0 = pt_("p0", [128, 512], F32)
        p1 = pt_("p1", [128, 512], F32)
        pbc = pt_("pbc", [128, 512], F32)
        pbc2 = pt_("pbc2", [128, 512], F32)
        pstats = pt_("pstats", [128, 512], F32)
        paux = pt_("paux", [128, 512], F32)
        rot = Rot([p0, p1])

        for c2 in range(NCH):
            s.dma("sync", vec_sb[:, c2:c2 + 1], lng_d[c2 * 128:(c2 + 1) * 128, :])
            s.dma("sync", vec_sb[:, 8 + c2:9 + c2], lnb_d[c2 * 128:(c2 + 1) * 128, :])
            s.dma("sync", vec_sb[:, 16 + c2:17 + c2], loglr_d[c2 * 128:(c2 + 1) * 128, :])
            s.dma("sync", vec_sb[:, 24 + c2:25 + c2], lsg_d[c2 * 128:(c2 + 1) * 128, :])
        s.op("gpsimd", lambda e: e.memset(ones1_b[:], 1.0))
        s.op("gpsimd", lambda e: e.memset(onesr[:], 1.0))
        s.op("gpsimd", lambda e: e.memset(ones_f[:], 1.0))
        s.op("gpsimd", lambda e: e.memset(carry[:], 0.0))
        s.op("gpsimd", lambda e: e.memset(epsr[:], 1e-5))
        stage = xt
        for wdram, wsb in [(wq_d, wq), (wk_d, wk), (wv_d, wv), (wg_d, wg),
                           (wo_d, wo), (w0_d, w0)]:
            load_cast_weight(s, wdram, wsb, stage, C, NCH)
        load_cast_weight(s, a_d, a_sb, stage, 8, NCH)
        tl = s.dma("sync", stage[0:8, 0:C], b_d[:, :])
        s.op("gpsimd", lambda e: e.tensor_copy(out=bin_sb[:], in_=stage[0:8, 0:C]),
             deps=[tl], inc="pg")
        s.barrier()
        te = s.op("scalar", lambda e: e.activation(
            out=vec_sb[:, 16:24], in_=vec_sb[:, 16:24], func=AF.Exp), inc="ps")
        s.op("vector", lambda e: e.tensor_scalar(
            out=vec_sb[:, 16:24], in0=vec_sb[:, 16:24], scalar1=1.0,
            scalar2=None, op0=ALU.min), deps=[te])
        s.op("vector", lambda e: e.tensor_copy(out=lnb_bf[:], in_=vec_sb[:, 8:16]))
        s.op("vector", lambda e: e.tensor_copy(out=lng_col[:], in_=vec_sb[:, 0:8]))
        s.barrier()
        for wsb, col0 in [(wq, 0), (wk, 8), (wv, 16)]:
            for j in range(NCH):
                pairs = [(wsb[:, c2 * C + j * 128:c2 * C + (j + 1) * 128],
                          lnb_bf[:, c2:c2 + 1]) for c2 in range(NCH)]
                tmm = mm_group(s, paux[:, 0:1], pairs)
                tev = s.op("vector", lambda e, col0=col0, j=j: e.tensor_copy(
                    out=biasqkv[:, col0 + j:col0 + j + 1], in_=paux[:, 0:1]),
                    deps=[tmm], inc="pv")
                s.wait("tensor", tev)
        s.barrier()
        for wsb in (wq, wk, wv):
            for c2 in range(NCH):
                s.op("vector", lambda e, wsb=wsb, c2=c2: e.tensor_scalar(
                    out=wsb[:, c2 * C:(c2 + 1) * C], in0=wsb[:, c2 * C:(c2 + 1) * C],
                    scalar1=lng_col[:, c2:c2 + 1], scalar2=None, op0=ALU.mult))
        s.barrier()

        R = lambda i: rows[:, i * TT:(i + 1) * TT]
        rM1, rR1, rRHO, rONEPR, rMR, rVAR2 = R(0), R(1), R(2), R(3), R(4), R(5)
        for it in range(NT):
            t0 = it * TT
            tl = None
            for c2 in range(NCH):
                tl = s.dma("sync", xt[:, c2 * TT:(c2 + 1) * TT],
                           x_d[c2 * 128:(c2 + 1) * 128, t0:t0 + TT])
            s.dma("sync", posb_t[:], posb_d[:, t0:t0 + TT])
            tc = None
            for c2 in range(NCH):
                tc = s.op("gpsimd", lambda e, c2=c2: e.tensor_copy(
                    out=x_bf[:, c2 * TT:(c2 + 1) * TT],
                    in_=xt[:, c2 * TT:(c2 + 1) * TT]), deps=[tl],
                    inc="pg" if c2 == NCH - 1 else None)
            tv = ln_stats(s, x_bf, sq_b, ones1_b,
                          pstats[0:1, 0:TT], pstats[0:1, TT:2 * TT],
                          rM1, rR1, TT, tc)
            s.op("vector", lambda e: e.tensor_copy(out=rVAR2, in_=rR1), deps=[tv])
            t_r1 = rstd_row(s, rR1, rR1, tv, epsr[0:1, 0:1])
            s.op("vector", lambda e: e.tensor_tensor(
                out=rRHO, in0=rR1, in1=rR1, op=ALU.mult), deps=[t_r1])
            tvv = s.op("vector", lambda e: e.tensor_tensor(
                out=rVAR2, in0=rVAR2, in1=rRHO, op=ALU.mult), inc="pv")
            tsq2 = s.op("scalar", lambda e: e.activation(
                out=rRHO, in_=rVAR2, func=AF.Sqrt, bias=epsr[0:1, 0:1],
                scale=1.0), deps=[tvv], inc="ps")
            s.op("vector", lambda e: e.reciprocal(out=rRHO, in_=rRHO), deps=[tsq2])
            s.op("vector", lambda e: e.tensor_tensor(
                out=rRHO, in0=rRHO, in1=rR1, op=ALU.mult))
            s.op("vector", lambda e: e.tensor_scalar(
                out=rONEPR, in0=rR1, scalar1=1.0, scalar2=None, op0=ALU.add))
            trows = s.op("vector", lambda e: e.tensor_tensor(
                out=rMR, in0=rM1, in1=rR1, op=ALU.mult), inc="pv")
            tb1 = bcast(s, pbc[:, 0:TT], rM1[0:1, 0:TT], onesr, [trows])
            tb2 = bcast(s, pbc[:, 256:256 + TT], rR1[0:1, 0:TT], onesr, [])
            tb3 = bcast(s, pbc2[:, 0:TT], rRHO[0:1, 0:TT], onesr, [])
            tap = None
            for c2 in range(NCH):
                s.op("vector", lambda e, c2=c2: e.tensor_tensor(
                    out=xmm[:, c2 * TT:(c2 + 1) * TT],
                    in0=xt[:, c2 * TT:(c2 + 1) * TT], in1=pbc[:, 0:TT],
                    op=ALU.subtract), deps=[tb2])
                s.op("vector", lambda e, c2=c2: e.tensor_tensor(
                    out=xin_b[:, c2 * TT:(c2 + 1) * TT],
                    in0=xmm[:, c2 * TT:(c2 + 1) * TT], in1=pbc[:, 256:256 + TT],
                    op=ALU.mult))
                tap = s.op("vector", lambda e, c2=c2: e.tensor_tensor(
                    out=h_b[:, c2 * TT:(c2 + 1) * TT],
                    in0=xmm[:, c2 * TT:(c2 + 1) * TT], in1=pbc2[:, 0:TT],
                    op=ALU.mult), deps=[tb3],
                    inc="pv" if c2 == NCH - 1 else None)
            s.barrier()

            def proj(wsb, rhs_buf, evac):
                for j in range(NCH):
                    ps, war = rot.acquire()
                    pairs = [(wsb[:, c2 * C + j * 128:c2 * C + (j + 1) * 128],
                              rhs_buf[:, c2 * TT:(c2 + 1) * TT])
                             for c2 in range(NCH)]
                    tmm = mm_group(s, ps[:, 0:TT], pairs, war_tok=war)
                    rot.release(evac(j, ps, tmm))

            def evq(j, ps, tok):
                return s.op("vector", lambda e, j=j, ps=ps: e.tensor_scalar(
                    out=q_f[:, j * TT:(j + 1) * TT], in0=ps[:, 0:TT],
                    scalar1=biasqkv[:, j:j + 1], scalar2=None, op0=ALU.add),
                    deps=[tok], inc="pv")
            proj(wq, h_b, evq)

            def evk(j, ps, tok):
                tkf = s.op("vector", lambda e, j=j, ps=ps: e.tensor_scalar(
                    out=k_f[:, j * TT:(j + 1) * TT], in0=ps[:, 0:TT],
                    scalar1=biasqkv[:, 8 + j:9 + j], scalar2=None, op0=ALU.add),
                    deps=[tok], inc="pv")
                s.op("gpsimd", lambda e, j=j: e.tensor_copy(
                    out=k_b[:, j * TT:(j + 1) * TT],
                    in_=k_f[:, j * TT:(j + 1) * TT]), deps=[tkf], inc="pg")
                return tkf
            proj(wk, h_b, evk)

            def evv(j, ps, tok):
                return s.op("vector", lambda e, j=j, ps=ps: e.tensor_scalar(
                    out=v_f[:, j * TT:(j + 1) * TT], in0=ps[:, 0:TT],
                    scalar1=biasqkv[:, 16 + j:17 + j], scalar2=None, op0=ALU.add),
                    deps=[tok], inc="pv")
            proj(wv, h_b, evv)

            def evg(j, ps, tok):
                return s.op("scalar", lambda e, j=j, ps=ps: e.activation(
                    out=gate_f[:, j * TT:(j + 1) * TT], in_=ps[:, 0:TT],
                    func=AF.Sigmoid), deps=[tok], inc="ps")
            proj(wg, xin_b, evg)
            s.barrier()

            pairs = [(a_sb[:, c2 * 8:(c2 + 1) * 8],
                      k_b[:, c2 * TT:(c2 + 1) * TT]) for c2 in range(NCH)]
            tka = mm_group(s, paux[0:8, 0:TT], pairs)
            tkab = s.op("vector", lambda e: e.tensor_copy(
                out=ka_b[:, 0:TT], in_=paux[0:8, 0:TT]), deps=[tka], inc="pv")
            for j in range(NCH):
                ps, war = rot.acquire()
                pairs = [(w0[:, c2 * C + j * 128:c2 * C + (j + 1) * 128],
                          k_b[:, c2 * TT:(c2 + 1) * TT]) for c2 in range(NCH)]
                pairs.append((bin_sb[0:8, j * 128:(j + 1) * 128], ka_b[0:8, 0:TT]))
                tmm = mm_group(s, ps[:, 0:TT], pairs, war_tok=war, extra_dep=tkab)
                tev = s.op("vector", lambda e, j=j, ps=ps: e.tensor_copy(
                    out=y0_f[:, j * TT:(j + 1) * TT], in_=ps[:, 0:TT]),
                    deps=[tmm], inc="pv")
                rot.release(tev)
            s.barrier()

            for c2 in range(NCH):
                sl = slice(c2 * TT, (c2 + 1) * TT)
                s.op("vector", lambda e, sl=sl: e.tensor_tensor(
                    out=grad_f[:, sl], in0=y0_f[:, sl], in1=v_f[:, sl],
                    op=ALU.subtract))
                s.op("vector", lambda e, sl=sl: e.tensor_tensor(
                    out=grad_f[:, sl], in0=grad_f[:, sl], in1=k_f[:, sl],
                    op=ALU.mult))
            for c2 in range(NCH):
                sl = slice(c2 * TT, (c2 + 1) * TT)
                s.op("vector", lambda e, sl=sl, c2=c2: e.tensor_tensor_scan(
                    out=gs_f[:, sl], data0=ones_f[:, 0:TT], data1=grad_f[:, sl],
                    initial=carry[:, c2:c2 + 1], op0=ALU.mult, op1=ALU.add))
                s.op("vector", lambda e, sl=sl, c2=c2: e.tensor_copy(
                    out=carry[:, c2:c2 + 1],
                    in_=gs_f[:, (c2 + 1) * TT - 1:(c2 + 1) * TT]))
                s.op("vector", lambda e, sl=sl: e.tensor_tensor(
                    out=gs_f[:, sl], in0=gs_f[:, sl], in1=grad_f[:, sl],
                    op=ALU.subtract))
            tg = None
            for c2 in range(NCH):
                sl = slice(c2 * TT, (c2 + 1) * TT)
                s.op("vector", lambda e, sl=sl, c2=c2: e.tensor_scalar(
                    out=q_f[:, sl], in0=q_f[:, sl],
                    scalar1=vec_sb[:, 16 + c2:17 + c2], scalar2=None,
                    op0=ALU.mult))
                s.op("vector", lambda e, sl=sl: e.tensor_tensor(
                    out=q_f[:, sl], in0=q_f[:, sl], in1=gs_f[:, sl], op=ALU.mult))
                s.op("vector", lambda e, sl=sl: e.tensor_tensor(
                    out=q_f[:, sl], in0=q_f[:, sl], in1=posb_t[:, 0:TT],
                    op=ALU.mult))
                s.op("vector", lambda e, sl=sl: e.tensor_tensor(
                    out=q_f[:, sl], in0=y0_f[:, sl], in1=q_f[:, sl],
                    op=ALU.subtract))
                tg = s.op("vector", lambda e, sl=sl: e.tensor_tensor(
                    out=outm_b[:, sl], in0=q_f[:, sl], in1=gate_f[:, sl],
                    op=ALU.mult), inc="pv" if c2 == NCH - 1 else None)
            s.barrier()
            fin_f = gs_f
            tb4 = bcast(s, pbc[:, 0:TT], rONEPR[0:1, 0:TT], onesr, [])
            tb5 = bcast(s, pbc[:, 256:256 + TT], rMR[0:1, 0:TT], onesr, [])
            for c2 in range(NCH):
                sl = slice(c2 * TT, (c2 + 1) * TT)
                s.op("vector", lambda e, sl=sl: e.tensor_tensor(
                    out=fin_f[:, sl], in0=xt[:, sl], in1=pbc[:, 0:TT],
                    op=ALU.mult), deps=[tb5])
                s.op("vector", lambda e, sl=sl: e.tensor_tensor(
                    out=fin_f[:, sl], in0=fin_f[:, sl], in1=pbc[:, 256:256 + TT],
                    op=ALU.subtract))
            for j in range(NCH):
                ps, war = rot.acquire()
                pairs = [(wo[:, c2 * C + j * 128:c2 * C + (j + 1) * 128],
                          outm_b[:, c2 * TT:(c2 + 1) * TT]) for c2 in range(NCH)]
                tmm = mm_group(s, ps[:, 0:TT], pairs, war_tok=war)
                s.op("vector", lambda e, j=j, ps=ps: e.tensor_scalar(
                    out=xmm[:, j * TT:(j + 1) * TT], in0=ps[:, 0:TT],
                    scalar1=vec_sb[:, 24 + j:25 + j], scalar2=None, op0=ALU.mult),
                    deps=[tmm])
                tfin = s.op("vector", lambda e, j=j: e.tensor_tensor(
                    out=fin_f[:, j * TT:(j + 1) * TT],
                    in0=fin_f[:, j * TT:(j + 1) * TT],
                    in1=xmm[:, j * TT:(j + 1) * TT], op=ALU.add), inc="pv")
                rot.release(tfin)
                s.dma("gpsimd", out_d[j * 128:(j + 1) * 128, t0:t0 + TT],
                      fin_f[:, j * TT:(j + 1) * TT], deps=[tfin])
            s.barrier()
        s.finalize()
    return nc





def build_l0f(T_loc=2048, TT=512, dbg=False):
    NT = T_loc // TT
    DF = 4096
    NF = DF // 128
    RK = 256
    nc = bass.Bass("TRN2", target_bir_lowering=False)
    x_d = f32in(nc, "x", [C, T_loc])
    u1_d = f32in(nc, "u1", [C, RK])
    v1_d = f32in(nc, "v1", [RK, DF])
    b1_d = f32in(nc, "b1", [DF, 1])
    u2_d = f32in(nc, "u2", [DF, RK])
    v2_d = f32in(nc, "v2", [RK, C])
    b2_d = f32in(nc, "b2", [C, 1])
    out_d = outap(nc, "xout", [C, T_loc])
    dbg_d = {}
    if dbg:
        dbg_d["xn"] = outap(nc, "dbg_xn", [128, NCH * TT])
        dbg_d["p"] = outap(nc, "dbg_p", [128, 2 * TT])
        dbg_d["h"] = outap(nc, "dbg_h", [128, 4 * TT])
        dbg_d["q2"] = outap(nc, "dbg_q2", [128, 2 * TT])

    with ExitStack() as ctx:
        sb = lambda name, shape, dt: ctx.enter_context(nc.sbuf_tensor(name, shape, dt))
        pt_ = lambda name, shape, dt: ctx.enter_context(nc.psum_tensor(name, shape, dt))
        s = Sched(nc, ctx)
        for nm in ("pt", "pv", "ps", "pg"):
            s.sem(nm)

        u1 = sb("u1s", [128, NCH * RK], BF16)
        v1 = sb("v1s", [128, 2 * DF], BF16)
        u2 = sb("u2s", [128, NF * RK], BF16)
        v2 = sb("v2s", [128, 2 * C], BF16)
        b1c = sb("b1c", [128, NF], F32)
        b2c = sb("b2c", [128, NCH], F32)
        ones1_b = sb("ones1_b", [128, 4], BF16)
        onesr = sb("onesr", [1, 128], F32)
        rows = sb("rows", [1, 2 * TT], F32)
        epsr = sb("epsr", [1, 2], F32)
        stage = sb("stage", [128, 2 * DF], F32)

        xt = sb("xt", [128, NCH * TT], F32)
        x_bf = sb("x_bfT", [128, NCH * TT], BF16)
        sq_bf = sb("sq_bfT", [128, NCH * TT], BF16)
        xn_b = sb("xn_b", [128, NCH * TT], BF16)
        p_b = sb("p_b", [128, 2 * TT], BF16)
        h_bf = sb("h_bf", [128, NF * TT], BF16)
        q2_b = sb("q2_b", [128, 2 * TT], BF16)

        p0 = pt_("p0", [128, 512], F32)
        p1 = pt_("p1", [128, 512], F32)
        pbc = pt_("pbc", [128, 512], F32)
        prb = pt_("prb", [128, 512], F32)
        pm = pt_("pm", [128, 512], F32)
        psq = pt_("psq", [128, 512], F32)
        rot = Rot([p0, p1])

        load_cast_weight(s, u1_d, u1, stage, RK, NCH)
        load_cast_weight(s, v1_d, v1, stage, DF, 2)
        load_cast_weight(s, u2_d, u2, stage, RK, NF)
        load_cast_weight(s, v2_d, v2, stage, C, 2)
        for j in range(NF):
            s.dma("sync", b1c[:, j:j + 1], b1_d[j * 128:(j + 1) * 128, :])
        for j in range(NCH):
            s.dma("sync", b2c[:, j:j + 1], b2_d[j * 128:(j + 1) * 128, :])
        s.op("gpsimd", lambda e: e.memset(ones1_b[:], 1.0))
        s.op("gpsimd", lambda e: e.memset(onesr[:], 1.0))
        s.op("gpsimd", lambda e: e.memset(epsr[:], 1e-5))
        s.barrier()

        rM, rR = rows[:, 0:TT], rows[:, TT:2 * TT]
        for it in range(NT):
            t0 = it * TT
            tl = None
            for c2 in range(NCH):
                tl = s.dma("sync", xt[:, c2 * TT:(c2 + 1) * TT],
                           x_d[c2 * 128:(c2 + 1) * 128, t0:t0 + TT])
            tc = None
            for c2 in range(NCH):
                tc = s.op("gpsimd", lambda e, c2=c2: e.tensor_copy(
                    out=x_bf[:, c2 * TT:(c2 + 1) * TT],
                    in_=xt[:, c2 * TT:(c2 + 1) * TT]), deps=[tl],
                    inc="pg" if c2 == NCH - 1 else None)
            tv = ln_stats(s, x_bf, sq_bf, ones1_b, pm[0:1, 0:TT],
                          psq[0:1, 0:TT], rM, rR, TT, tc)
            trs = rstd_row(s, rR, rR, tv, epsr[0:1, 0:1])
            tb1 = bcast(s, pbc[:, 0:TT], rM[0:1, 0:TT], onesr, [trs])
            tb2 = bcast(s, prb[:, 0:TT], rR[0:1, 0:TT], onesr, [])
            tap = None
            for c2 in range(NCH):
                sl = slice(c2 * TT, (c2 + 1) * TT)
                s.op("vector", lambda e, sl=sl: e.tensor_tensor(
                    out=xt[:, sl], in0=xt[:, sl], in1=pbc[:, 0:TT],
                    op=ALU.subtract), deps=[tb1, tb2])
                tap = s.op("vector", lambda e, sl=sl: e.tensor_tensor(
                    out=xn_b[:, sl], in0=xt[:, sl], in1=prb[:, 0:TT],
                    op=ALU.mult), inc="pv" if c2 == NCH - 1 else None)
            s.barrier()
            for j in range(2):
                ps, war = rot.acquire()
                pairs = [(u1[:, c2 * RK + j * 128:c2 * RK + (j + 1) * 128],
                          xn_b[:, c2 * TT:(c2 + 1) * TT]) for c2 in range(NCH)]
                tmm = mm_group(s, ps[:, 0:TT], pairs, war_tok=war)
                tev = s.op("vector", lambda e, j=j, ps=ps: e.tensor_copy(
                    out=p_b[:, j * TT:(j + 1) * TT], in_=ps[:, 0:TT]),
                    deps=[tmm], inc="pv")
                rot.release(tev)
            s.barrier()
            for j in range(NF):
                ps, war = rot.acquire()
                pairs = [(v1[:, c2 * DF + j * 128:c2 * DF + (j + 1) * 128],
                          p_b[:, c2 * TT:(c2 + 1) * TT]) for c2 in range(2)]
                tmm = mm_group(s, ps[:, 0:TT], pairs, war_tok=war)
                tev = s.op("scalar", lambda e, j=j, ps=ps: e.activation(
                    out=h_bf[:, j * TT:(j + 1) * TT], in_=ps[:, 0:TT],
                    func=AF.Gelu_apprx_tanh, bias=b1c[:, j:j + 1], scale=1.0),
                    deps=[tmm], inc="ps")
                rot.release(tev)
            s.barrier()
            for j in range(2):
                ps, war = rot.acquire()
                pairs = [(u2[:, c2 * RK + j * 128:c2 * RK + (j + 1) * 128],
                          h_bf[:, c2 * TT:(c2 + 1) * TT]) for c2 in range(NF)]
                tmm = mm_group(s, ps[:, 0:TT], pairs, war_tok=war)
                tev = s.op("vector", lambda e, j=j, ps=ps: e.tensor_copy(
                    out=q2_b[:, j * TT:(j + 1) * TT], in_=ps[:, 0:TT]),
                    deps=[tmm], inc="pv")
                rot.release(tev)
            s.barrier()
            if dbg and it == 0:
                s.barrier()
                s.dma("gpsimd", dbg_d["xn"], xn_b[:, 0:NCH * TT])
                s.dma("gpsimd", dbg_d["p"], p_b[:, 0:2 * TT])
                s.dma("gpsimd", dbg_d["h"], h_bf[:, 0:4 * TT])
                s.dma("gpsimd", dbg_d["q2"], q2_b[:, 0:2 * TT])
                s.barrier()
            tl2 = None
            for c2 in range(NCH):
                tl2 = s.dma("sync", xt[:, c2 * TT:(c2 + 1) * TT],
                            x_d[c2 * 128:(c2 + 1) * 128, t0:t0 + TT])
            for j in range(NCH):
                ps, war = rot.acquire()
                pairs = [(v2[:, c2 * C + j * 128:c2 * C + (j + 1) * 128],
                          q2_b[:, c2 * TT:(c2 + 1) * TT]) for c2 in range(2)]
                tmm = mm_group(s, ps[:, 0:TT], pairs, war_tok=war)
                tfin = s.op("vector", lambda e, j=j, ps=ps: e.scalar_tensor_tensor(
                    out=xt[:, j * TT:(j + 1) * TT], in0=ps[:, 0:TT],
                    scalar=b2c[:, j:j + 1], in1=xt[:, j * TT:(j + 1) * TT],
                    op0=ALU.add, op1=ALU.add), deps=[tmm, tl2], inc="pv")
                rot.release(tfin)
                s.dma("gpsimd", out_d[j * 128:(j + 1) * 128, t0:t0 + TT],
                      xt[:, j * TT:(j + 1) * TT], deps=[tfin])
            s.barrier()
        s.finalize()
    return nc





def build_l2(TT=512, dbg=False):
    NT = T_FULL // TT
    nc = bass.Bass("TRN2", target_bir_lowering=False)
    x_d = f32in(nc, "x", [C, T_FULL])
    xtm_d = f32in(nc, "xtm", [T_FULL, C])
    wq_d = f32in(nc, "wq", [C, C])
    wk_d = f32in(nc, "wk", [C, C])
    wv_d = f32in(nc, "wv", [C, C])
    ident_d = bf16in(nc, "ident", [128, 128])
    out_d = outap(nc, "xout", [T_FULL, C])
    k_scr = nc.dram_tensor("k_scr", [C, T_FULL], BF16, kind="Internal").ap()
    v_scr = nc.dram_tensor("v_scr", [T_FULL, C], BF16, kind="Internal").ap()
    dbg_d = {}
    if dbg:
        dbg_d["sf"] = outap(nc, "dbg_sf", [128, 1024])
        dbg_d["mx"] = outap(nc, "dbg_mx", [128, 8])
        dbg_d["z"] = outap(nc, "dbg_z", [128, 8])
        dbg_d["pb"] = outap(nc, "dbg_pb", [128, 1024])
        dbg_d["q"] = outap(nc, "dbg_q", [128, 1024])
        dbg_d["k"] = outap(nc, "dbg_k", [128, 1024])

    with ExitStack() as ctx:
        sb = lambda name, shape, dt: ctx.enter_context(nc.sbuf_tensor(name, shape, dt))
        pt_ = lambda name, shape, dt: ctx.enter_context(nc.psum_tensor(name, shape, dt))
        s = Sched(nc, ctx)

        q_sb = sb("q_sb", [128, NCH * T_FULL], BF16)
        ident = sb("ident_sb", [128, 128], BF16)
        onesr = sb("onesr", [1, 128], F32)
        ones1_b = sb("ones1_b", [128, 4], BF16)
        epsr = sb("epsr", [1, 2], F32)
        rows = sb("rows", [1, 2 * TT], F32)
        actx = ExitStack()
        sba = lambda name, shape, dt: actx.enter_context(nc.sbuf_tensor(name, shape, dt))
        wq = sba("wq_sb", [128, NCH * C], BF16)
        wk = sba("wk_sb", [128, NCH * C], BF16)
        wv = sba("wv_sb", [128, NCH * C], BF16)
        stage = sba("stage", [128, 2 * C], F32)
        xt = sba("xt", [128, NCH * TT], F32)
        x_bf = sba("x_bfT", [128, NCH * TT], BF16)
        sq_bf = sba("sq_bfT", [128, NCH * TT], BF16)
        xn_b = sba("xn_b", [128, NCH * TT], BF16)

        p0 = pt_("p0", [128, 512], F32)
        p1 = pt_("p1", [128, 512], F32)
        pbc = actx.enter_context(nc.psum_tensor("pbc", [128, 512], F32))
        prb = actx.enter_context(nc.psum_tensor("prb", [128, 512], F32))
        pm = actx.enter_context(nc.psum_tensor("pm", [128, 512], F32))
        psq = actx.enter_context(nc.psum_tensor("psq", [128, 512], F32))
        rot = Rot([p0, p1])

        s.dma("sync", ident[:], ident_d[:])
        s.op("gpsimd", lambda e: e.memset(onesr[:], 1.0))
        s.op("gpsimd", lambda e: e.memset(ones1_b[:], 1.0))
        s.op("gpsimd", lambda e: e.memset(epsr[:], 1e-5))
        load_cast_weight(s, wq_d, wq, stage, C, NCH)
        load_cast_weight(s, wk_d, wk, stage, C, NCH)
        load_cast_weight(s, wv_d, wv, stage, C, NCH)

        rM, rR = rows[:, 0:TT], rows[:, TT:2 * TT]
        for it in range(NT):
            t0 = it * TT
            tl = None
            for c2 in range(NCH):
                tl = s.dma("sync", xt[:, c2 * TT:(c2 + 1) * TT],
                           x_d[c2 * 128:(c2 + 1) * 128, t0:t0 + TT])
            tc = None
            for c2 in range(NCH):
                tc = s.op("gpsimd", lambda e, c2=c2: e.tensor_copy(
                    out=x_bf[:, c2 * TT:(c2 + 1) * TT],
                    in_=xt[:, c2 * TT:(c2 + 1) * TT]), deps=[tl])
            tv = ln_stats(s, x_bf, sq_bf, ones1_b, pm[0:1, 0:TT],
                          psq[0:1, 0:TT], rM, rR, TT, tc)
            trs = rstd_row(s, rR, rR, tv, epsr[0:1, 0:1])
            tb1 = bcast(s, pbc[:, 0:TT], rM[0:1, 0:TT], onesr, [trs])
            tb2 = bcast(s, prb[:, 0:TT], rR[0:1, 0:TT], onesr, [])
            for c2 in range(NCH):
                sl = slice(c2 * TT, (c2 + 1) * TT)
                s.op("vector", lambda e, sl=sl: e.tensor_tensor(
                    out=xt[:, sl], in0=xt[:, sl], in1=pbc[:, 0:TT],
                    op=ALU.subtract), deps=[tb1, tb2])
                s.op("vector", lambda e, sl=sl: e.tensor_tensor(
                    out=xn_b[:, sl], in0=xt[:, sl], in1=prb[:, 0:TT],
                    op=ALU.mult))
            s.barrier()
            for wsb, dst in ((wq, None), (wk, None)):
                for j in range(NCH):
                    ps, war = rot.acquire()
                    pairs = [(wsb[:, c2 * C + j * 128:c2 * C + (j + 1) * 128],
                              xn_b[:, c2 * TT:(c2 + 1) * TT])
                             for c2 in range(NCH)]
                    tmm = mm_group(s, ps[:, 0:TT], pairs, war_tok=war)
                    if wsb is wq:
                        tev = s.op("vector", lambda e, j=j, ps=ps, t0=t0: e.tensor_copy(
                            out=q_sb[:, j * T_FULL + t0:j * T_FULL + t0 + TT],
                            in_=ps[:, 0:TT]), deps=[tmm])
                        rot.release(tev)
                    else:
                        tev = s.op("vector", lambda e, j=j, ps=ps: e.tensor_copy(
                            out=x_bf[:, j * TT:(j + 1) * TT], in_=ps[:, 0:TT]),
                            deps=[tmm])
                        tst = s.dma("gpsimd",
                                    k_scr[j * 128:(j + 1) * 128, t0:t0 + TT],
                                    x_bf[:, j * TT:(j + 1) * TT], deps=[tev])
                        rot.release(tev)
            s.barrier()
            for tsub in range(TT // 128):
                for half in range(2):
                    ps, war = rot.acquire()
                    pairs = [(xn_b[:, c2 * TT + tsub * 128:c2 * TT + (tsub + 1) * 128],
                              wv[:, c2 * C + half * 512:c2 * C + (half + 1) * 512])
                             for c2 in range(NCH)]
                    tmm = mm_group(s, ps[:, 0:512], pairs, war_tok=war)
                    tev = s.op("vector", lambda e, ps=ps, tsub=tsub, half=half:
                               e.tensor_copy(out=sq_bf[:, tsub * 1024 + half * 512:
                                             tsub * 1024 + (half + 1) * 512],
                                             in_=ps[:, 0:512]), deps=[tmm])
                    rot.release(tev)
                    s.dma("gpsimd",
                          v_scr[t0 + tsub * 128:t0 + (tsub + 1) * 128,
                                half * 512:(half + 1) * 512],
                          sq_bf[:, tsub * 1024 + half * 512:tsub * 1024 + (half + 1) * 512],
                          deps=[tev])
            s.barrier()
        s.barrier()

        actx.close()
        kbuf = sb("kbuf", [128, NCH * 512], BF16)
        vbuf = sb("vbuf", [128, C], BF16)
        sf = sb("sf", [128, T_FULL], F32)
        pbf = sb("pbf", [128, T_FULL], BF16)
        ptr_sb = sb("ptr_sb", [128, 128], BF16)
        ymx = sb("ymx", [128, 8], F32)
        zrow = sb("zrow", [128, 8], F32)
        xtmt = sb("xtmt", [128, C], F32)
        yout = sb("yout", [128, C], F32)
        pst = pt_("pst", [128, 1024], BF16)
        py0 = pt_("py0", [128, 512], F32)
        py1 = pt_("py1", [128, 512], F32)

        for st_i in range(T_FULL // 128):
            qs0 = st_i * 128
            nkt = (qs0 + 128 + 511) // 512
            tmax_deps = []
            prev_smm = None
            for kt in range(nkt):
                k0 = kt * 512
                kw = min(512, T_FULL - k0)
                tlk = None
                for c2 in range(NCH):
                    tlk = s.dma("sync", kbuf[:, c2 * 512:c2 * 512 + kw],
                                k_scr[c2 * 128:(c2 + 1) * 128, k0:k0 + kw],
                                deps=[prev_smm] if prev_smm else [])
                ps, war = rot.acquire()
                pairs = [(q_sb[:, c2 * T_FULL + qs0:c2 * T_FULL + qs0 + 128],
                          kbuf[:, c2 * 512:c2 * 512 + kw]) for c2 in range(NCH)]
                tmm = mm_group(s, ps[:, 0:kw], pairs, war_tok=war, extra_dep=tlk)
                prev_smm = tmm
                tev = s.op("scalar", lambda e, ps=ps, k0=k0, kw=kw: e.mul(
                    out=sf[:, k0:k0 + kw], in_=ps[:, 0:kw], mul=0.03125),
                    deps=[tmm])
                rot.release(tev)
                if k0 + kw > qs0:
                    tev = s.op("gpsimd", lambda e, k0=k0, kw=kw, qs0=qs0:
                               e.affine_select(
                        out=sf[:, k0:k0 + kw], in_=sf[:, k0:k0 + kw],
                        compare_op=ALU.is_ge, fill=-1e9,
                        pattern=[[-1, kw]], base=qs0 - k0,
                        channel_multiplier=1), deps=[tev])
                tmax_deps.append(tev)
            ext = nkt * 512
            ext = min(ext, T_FULL)
            tm = s.op("vector", lambda e, ext=ext: e.reduce_max(
                out=ymx[:, 0:1], in_=sf[:, 0:ext], axis=AX.X),
                deps=tmax_deps)
            tneg = s.op("vector", lambda e: e.tensor_scalar(
                out=ymx[:, 1:2], in0=ymx[:, 0:1], scalar1=-1.0, scalar2=None,
                op0=ALU.mult), deps=[tm])
            texp = s.op("scalar", lambda e, ext=ext: e.activation(
                out=pbf[:, 0:ext], in_=sf[:, 0:ext], func=AF.Exp,
                bias=ymx[:, 1:2], scale=1.0, accum_out=zrow[:, 0:1]),
                deps=[tneg])
            trec = s.op("vector", lambda e: e.reciprocal(
                out=zrow[:, 1:2], in_=zrow[:, 0:1]), deps=[texp])
            nkt128 = (qs0 + 128) // 128
            prev_pv = None
            prev_tcp = None
            for kt in range(nkt128):
                ttr = s.op("tensor", lambda e, kt=kt: e.transpose(
                    pst[:, 0:128], pbf[:, kt * 128:(kt + 1) * 128], ident[:]),
                    deps=[texp] + ([prev_tcp] if prev_tcp else []))
                tcp = s.op("vector", lambda e: e.tensor_copy(
                    out=ptr_sb[:], in_=pst[:, 0:128]),
                    deps=[ttr] + ([prev_pv] if prev_pv else []))
                prev_tcp = tcp
                tlv = s.dma("sync", vbuf[:], v_scr[kt * 128:(kt + 1) * 128, :],
                            deps=[prev_pv] if prev_pv else [])
                for half, psy in ((0, py0), (1, py1)):
                    prev_pv = s.op("tensor", lambda e, psy=psy, half=half, kt=kt:
                         e.matmul(psy[:, 0:512], ptr_sb[:],
                                  vbuf[:, half * 512:(half + 1) * 512],
                                  start=(kt == 0), stop=(kt == nkt128 - 1)),
                         deps=[tcp, tlv])
            tmm_last = ("pt", s.cnt["pt"])
            tlx = s.dma("sync", xtmt[:], xtm_d[qs0:qs0 + 128, :])
            for half, psy in ((0, py0), (1, py1)):
                tev = s.op("vector", lambda e, psy=psy, half=half: e.tensor_scalar(
                    out=yout[:, half * 512:(half + 1) * 512], in0=psy[:, 0:512],
                    scalar1=zrow[:, 1:2], scalar2=None, op0=ALU.mult),
                    deps=[tmm_last, trec])
                tfin = s.op("vector", lambda e, half=half: e.tensor_tensor(
                    out=yout[:, half * 512:(half + 1) * 512],
                    in0=yout[:, half * 512:(half + 1) * 512],
                    in1=xtmt[:, half * 512:(half + 1) * 512], op=ALU.add),
                    deps=[tlx])
            tst = s.dma("gpsimd", out_d[qs0:qs0 + 128, :], yout[:], deps=[tfin])
            if dbg and st_i == 3:
                s.barrier()
                s.dma("gpsimd", dbg_d["sf"], sf[:, 0:1024])
                s.dma("gpsimd", dbg_d["mx"], ymx[:, 0:8])
                s.dma("gpsimd", dbg_d["z"], zrow[:, 0:8])
                tcpb = s.op("vector", lambda e: e.tensor_copy(
                    out=xtmt[:, 0:1024], in_=pbf[:, 0:1024]))
                s.dma("gpsimd", dbg_d["pb"], xtmt[:, 0:1024], deps=[tcpb])
                tq = s.op("vector", lambda e: e.tensor_copy(
                    out=xtmt[:, 0:1024], in_=q_sb[:, 0:1024]), deps=[tcpb])
                s.dma("gpsimd", dbg_d["q"], xtmt[:, 0:1024], deps=[tq])
                tk = s.dma("sync", kbuf[:, 0:512], k_scr[0:128, 0:512])
                tk2 = s.op("vector", lambda e: e.tensor_copy(
                    out=xtmt[:, 0:512], in_=kbuf[:, 0:512]), deps=[tk, tq])
                s.dma("gpsimd", dbg_d["k"], xtmt[:, 0:1024], deps=[tk2])
            s.barrier()
        s.finalize()
    return nc


def run_l2(g, x):
    nc = _get("l2", build_l2)
    base = {
        "wq": np.ascontiguousarray(g["L2_attn_WQ"]),
        "wk": np.ascontiguousarray(g["L2_attn_WK"]),
        "wv": np.ascontiguousarray(g["L2_attn_WV"]),
        "ident": np.eye(128, dtype=BF),
    }
    in_maps = []
    for c in range(N_CORES):
        m = dict(base)
        m["x"] = _fm(x[c // 2])
        m["xtm"] = np.ascontiguousarray(x[c // 2])
        in_maps.append(m)
    res = run_bass_kernel_spmd(nc, in_maps, core_ids=list(range(N_CORES)))
    return np.ascontiguousarray(
        np.stack([res.results[2 * b]["xout"] for b in range(B)]))





def build_l3(TT=512):
    NT = T_FULL // TT
    nc = bass.Bass("TRN2", target_bir_lowering=False)
    x_d = f32in(nc, "x", [C, T_FULL])
    xtm_d = f32in(nc, "xtm", [T_FULL, C])
    wq_d = f32in(nc, "wq", [C, C])
    wk_d = f32in(nc, "wk", [C, C])
    wv_d = f32in(nc, "wv", [C, C])
    lng_d = f32in(nc, "lng", [C, 1])
    lnb_d = f32in(nc, "lnb", [C, 1])
    bq_d = f32in(nc, "bq", [C, 1])
    bk_d = f32in(nc, "bk", [C, 1])
    bv_d = f32in(nc, "bv", [C, 1])
    bvr_d = f32in(nc, "bvr", [1, C])
    ident_d = bf16in(nc, "ident", [128, 128])
    identf_d = f32in(nc, "identf", [128, 128])
    out_d = outap(nc, "xout", [T_FULL, C])
    q_scr = nc.dram_tensor("q_scr", [C, T_FULL], BF16, kind="Internal").ap()
    k_scr = nc.dram_tensor("k_scr", [C, T_FULL], BF16, kind="Internal").ap()
    v_scr = nc.dram_tensor("v_scr", [T_FULL, C], BF16, kind="Internal").ap()

    with ExitStack() as ctx:
        sb = lambda name, shape, dt: ctx.enter_context(nc.sbuf_tensor(name, shape, dt))
        pt_ = lambda name, shape, dt: ctx.enter_context(nc.psum_tensor(name, shape, dt))
        s = Sched(nc, ctx)

        ident = sb("ident_sb", [128, 128], BF16)
        identf = sb("identf_sb", [128, 128], F32)
        onesr = sb("onesr", [1, 128], F32)
        ones1_b = sb("ones1_b", [128, 4], BF16)
        onesr_b = sb("onesr_b", [1, 128], BF16)
        epsr = sb("epsr", [1, 2], F32)
        rows = sb("rows", [1, 4 * TT], F32)
        biasvr_b = sb("biasvr_b", [1, C], BF16)
        s.dma("sync", ident[:], ident_d[:])
        s.dma("sync", identf[:], identf_d[:])
        s.op("gpsimd", lambda e: e.memset(onesr[:], 1.0))
        s.op("gpsimd", lambda e: e.memset(ones1_b[:], 1.0))
        s.op("gpsimd", lambda e: e.memset(onesr_b[:], 1.0))
        s.op("gpsimd", lambda e: e.memset(epsr[:], 1e-5))

        actx = ExitStack()
        sba = lambda name, shape, dt: actx.enter_context(nc.sbuf_tensor(name, shape, dt))
        wq = sba("wq_sb", [128, NCH * C], BF16)
        wk = sba("wk_sb", [128, NCH * C], BF16)
        wv = sba("wv_sb", [128, NCH * C], BF16)
        stage = sba("stage", [128, 2 * C], F32)
        vcols = sba("vcols", [128, 40], F32)
        lnb_bf = sba("lnb_bf", [128, NCH], BF16)
        biasw = sba("biasw", [128, 16], F32)
        bvrow = sba("bvrow", [1, C], F32)
        xt = sba("xt", [128, NCH * TT], F32)
        x_bf = sba("x_bfT", [128, NCH * TT], BF16)
        sq_bf = sba("sq_bfT", [128, NCH * TT], BF16)
        xn_b = sba("xn_b", [128, NCH * TT], BF16)
        ev_b = sba("ev_b", [128, NCH * TT], BF16)
        pbc = actx.enter_context(nc.psum_tensor("pbc", [128, 512], F32))
        prb = actx.enter_context(nc.psum_tensor("prb", [128, 512], F32))
        pm = actx.enter_context(nc.psum_tensor("pm", [128, 512], F32))
        psq = actx.enter_context(nc.psum_tensor("psq", [128, 512], F32))
        p0 = actx.enter_context(nc.psum_tensor("p0", [128, 512], F32))
        p1 = actx.enter_context(nc.psum_tensor("p1", [128, 512], F32))
        rot = Rot([p0, p1])

        for c2 in range(NCH):
            s.dma("sync", vcols[:, c2:c2 + 1], lng_d[c2 * 128:(c2 + 1) * 128, :])
            s.dma("sync", vcols[:, 8 + c2:9 + c2], lnb_d[c2 * 128:(c2 + 1) * 128, :])
            s.dma("sync", vcols[:, 16 + c2:17 + c2], bq_d[c2 * 128:(c2 + 1) * 128, :])
            s.dma("sync", vcols[:, 24 + c2:25 + c2], bk_d[c2 * 128:(c2 + 1) * 128, :])
        s.dma("sync", bvrow[0:1, :], bvr_d[0:1, :])
        load_cast_weight(s, wq_d, wq, stage, C, NCH)
        load_cast_weight(s, wk_d, wk, stage, C, NCH)
        load_cast_weight(s, wv_d, wv, stage, C, NCH)
        s.op("vector", lambda e: e.tensor_copy(out=lnb_bf[:], in_=vcols[:, 8:16]))
        s.barrier()
        for wsb, col0, bcol in ((wq, 0, 16), (wk, 8, 24)):
            for j in range(NCH):
                pairs = [(wsb[:, c2 * C + j * 128:c2 * C + (j + 1) * 128],
                          lnb_bf[:, c2:c2 + 1]) for c2 in range(NCH)]
                ps, war = rot.acquire()
                tmm = mm_group(s, ps[:, 0:1], pairs, war_tok=war)
                tev = s.op("vector", lambda e, col0=col0, j=j, bcol=bcol, ps=ps:
                           e.scalar_tensor_tensor(
                    out=biasw[:, col0 + j:col0 + j + 1], in0=ps[:, 0:1],
                    scalar=1.0, in1=vcols[:, bcol + j:bcol + j + 1],
                    op0=ALU.mult, op1=ALU.add), deps=[tmm])
                rot.release(tev)
        for half in range(2):
            ps, war = rot.acquire()
            pairs = [(lnb_bf[:, c2:c2 + 1],
                      wv[:, c2 * C + half * 512:c2 * C + (half + 1) * 512])
                     for c2 in range(NCH)]
            tmm = mm_group(s, ps[0:1, 0:512], pairs, war_tok=war)
            tev = s.op("vector", lambda e, ps=ps, half=half: e.tensor_tensor(
                out=bvrow[0:1, half * 512:(half + 1) * 512], in0=ps[0:1, 0:512],
                in1=bvrow[0:1, half * 512:(half + 1) * 512], op=ALU.add),
                deps=[tmm])
            rot.release(tev)
        tbv = s.op("vector", lambda e: e.tensor_copy(
            out=biasvr_b[0:1, :], in_=bvrow[0:1, :]))
        s.barrier()
        for wsb in (wq, wk, wv):
            for c2 in range(NCH):
                s.op("vector", lambda e, wsb=wsb, c2=c2: e.tensor_scalar(
                    out=wsb[:, c2 * C:(c2 + 1) * C], in0=wsb[:, c2 * C:(c2 + 1) * C],
                    scalar1=vcols[:, c2:c2 + 1], scalar2=None, op0=ALU.mult))
        s.barrier()

        R = lambda i: rows[:, i * TT:(i + 1) * TT]
        rM, rR, rSIG, rT = R(0), R(1), R(2), R(3)
        for it in range(NT):
            t0 = it * TT
            tl = None
            for c2 in range(NCH):
                tl = s.dma("sync", xt[:, c2 * TT:(c2 + 1) * TT],
                           x_d[c2 * 128:(c2 + 1) * 128, t0:t0 + TT])
            tc = None
            for c2 in range(NCH):
                tc = s.op("gpsimd", lambda e, c2=c2: e.tensor_copy(
                    out=x_bf[:, c2 * TT:(c2 + 1) * TT],
                    in_=xt[:, c2 * TT:(c2 + 1) * TT]), deps=[tl])
            tv = ln_stats(s, x_bf, sq_bf, ones1_b, pm[0:1, 0:TT],
                          psq[0:1, 0:TT], rM, rR, TT, tc)
            s.op("vector", lambda e: e.tensor_copy(
                out=rT[0:1, 0:TT], in_=rR[0:1, 0:TT]), deps=[tv])
            tr1 = rstd_row(s, rR, rR, tv, epsr[0:1, 0:1])
            s.op("vector", lambda e: e.tensor_tensor(
                out=rSIG[0:1, 0:TT], in0=rR[0:1, 0:TT], in1=rR[0:1, 0:TT],
                op=ALU.mult), deps=[tr1])
            tvv = s.op("vector", lambda e: e.tensor_tensor(
                out=rT[0:1, 0:TT], in0=rT[0:1, 0:TT], in1=rSIG[0:1, 0:TT],
                op=ALU.mult))
            tsq2 = s.op("scalar", lambda e: e.activation(
                out=rSIG[0:1, 0:TT], in_=rT[0:1, 0:TT], func=AF.Sqrt,
                bias=epsr[0:1, 0:1], scale=1.0), deps=[tvv])
            s.op("vector", lambda e: e.reciprocal(
                out=rSIG[0:1, 0:TT], in_=rSIG[0:1, 0:TT]), deps=[tsq2])
            trows = s.op("vector", lambda e: e.tensor_tensor(
                out=rSIG[0:1, 0:TT], in0=rSIG[0:1, 0:TT], in1=rR[0:1, 0:TT],
                op=ALU.mult))
            tb1 = bcast(s, pbc[:, 0:TT], rM[0:1, 0:TT], onesr, [trows])
            tb2 = bcast(s, prb[:, 0:TT], rSIG[0:1, 0:TT], onesr, [])
            for c2 in range(NCH):
                sl = slice(c2 * TT, (c2 + 1) * TT)
                s.op("vector", lambda e, sl=sl: e.tensor_tensor(
                    out=xt[:, sl], in0=xt[:, sl], in1=pbc[:, 0:TT],
                    op=ALU.subtract), deps=[tb1, tb2])
                s.op("vector", lambda e, sl=sl: e.tensor_tensor(
                    out=xn_b[:, sl], in0=xt[:, sl], in1=prb[:, 0:TT],
                    op=ALU.mult))
            s.barrier()
            for wsb, scr, bc0 in ((wq, q_scr, 0), (wk, k_scr, 8)):
                for j in range(NCH):
                    ps, war = rot.acquire()
                    pairs = [(wsb[:, c2 * C + j * 128:c2 * C + (j + 1) * 128],
                              xn_b[:, c2 * TT:(c2 + 1) * TT])
                             for c2 in range(NCH)]
                    tmm = mm_group(s, ps[:, 0:TT], pairs, war_tok=war)
                    tev = s.op("vector", lambda e, j=j, ps=ps, bc0=bc0:
                               e.tensor_scalar(
                        out=ev_b[:, j * TT:(j + 1) * TT], in0=ps[:, 0:TT],
                        scalar1=biasw[:, bc0 + j:bc0 + j + 1], scalar2=None,
                        op0=ALU.add), deps=[tmm])
                    s.dma("gpsimd", scr[j * 128:(j + 1) * 128, t0:t0 + TT],
                          ev_b[:, j * TT:(j + 1) * TT], deps=[tev])
                    rot.release(tev)
            s.barrier()
            for tsub in range(TT // 128):
                for half in range(2):
                    ps, war = rot.acquire()
                    pairs = [(xn_b[:, c2 * TT + tsub * 128:c2 * TT + (tsub + 1) * 128],
                              wv[:, c2 * C + half * 512:c2 * C + (half + 1) * 512])
                             for c2 in range(NCH)]
                    tmm = mm_group(s, ps[:, 0:512], pairs, war_tok=war)
                    tbias = s.op("tensor", lambda e, ps=ps, half=half: e.matmul(
                        ps[:, 0:512], onesr_b[0:1, 0:128],
                        biasvr_b[0:1, half * 512:(half + 1) * 512],
                        start=False, stop=True), deps=[tbv])
                    tev = s.op("vector", lambda e, ps=ps, tsub=tsub, half=half:
                               e.tensor_copy(out=ev_b[:, tsub * 1024 + half * 512:
                                             tsub * 1024 + (half + 1) * 512],
                                             in_=ps[:, 0:512]), deps=[tbias])
                    rot.release(tev)
                    s.dma("gpsimd",
                          v_scr[t0 + tsub * 128:t0 + (tsub + 1) * 128,
                                half * 512:(half + 1) * 512],
                          ev_b[:, tsub * 1024 + half * 512:tsub * 1024 + (half + 1) * 512],
                          deps=[tev])
            s.barrier()
        actx.close()
        s.barrier()

        qhp = sb("qhp", [128, T_FULL], BF16)
        khp = sb("khp", [128, T_FULL], BF16)
        vhp = sb("vhp", [128, T_FULL], BF16)
        pbuf0 = sb("pbuf0", [128, 512], BF16)
        pbuf1 = sb("pbuf1", [128, 512], BF16)
        mxc = sb("mxc", [128, 32], F32)
        negrow = sb("negrow", [1, 1024], BF16)
        zrec = sb("zrec", [1, 1024], F32)
        zbr = sb("zbr", [128, 512], F32)
        y_fm = sb("y_fm", [128, NCH * T_FULL], BF16)
        pS0 = pt_("pS0", [128, 512], F32)
        pS1 = pt_("pS1", [128, 512], F32)
        pZ0 = pt_("pZ0", [128, 512], F32)
        pZ1 = pt_("pZ1", [128, 512], F32)
        pY0 = pt_("pY0", [128, 512], F32)
        pY1 = pt_("pY1", [128, 512], F32)
        NKT = T_FULL // 128

        prev = {"r0": None, "r1": None, "pv0": None, "pv1": None,
                "z0": None, "z1": None, "ev": None, "e0": None, "e1": None,
                "neg": None}
        for hp in range(NCH):
            dq = [prev["r0"], prev["r1"], prev["e0"], prev["e1"]]
            tq = s.dma("sync", qhp[:], q_scr[hp * 128:(hp + 1) * 128, :],
                       deps=[d for d in dq if d])
            tk = s.dma("sync", khp[:], k_scr[hp * 128:(hp + 1) * 128, :],
                       deps=[])
            tvl = None
            for kt in range(NKT):
                tvl = s.dma("sync", vhp[:, kt * 128:(kt + 1) * 128],
                            v_scr[kt * 128:(kt + 1) * 128, hp * 128:(hp + 1) * 128],
                            deps=[prev["pv0"], prev["pv1"]] if kt == 0 and prev["pv0"] else [])
            s.barrier()
            for qg in range(T_FULL // 512):
                q0 = qg * 512
                for sti in range(4):
                    qs = q0 + sti * 128
                    for kt in range(T_FULL // 512):
                        t1 = s.op("tensor", lambda e, qs=qs, kt=kt: e.matmul(
                            pS0[:, 0:512], qhp[0:64, qs:qs + 128],
                            khp[0:64, kt * 512:(kt + 1) * 512],
                            start=True, stop=True),
                            deps=[prev["r0"]] if prev["r0"] else [])
                        prev["r0"] = s.op("vector", lambda e, kt=kt: e.reduce_max(
                            out=mxc[:, kt:kt + 1], in_=pS0[:, 0:512], axis=AX.X),
                            deps=[t1])
                        t2 = s.op("tensor", lambda e, qs=qs, kt=kt: e.matmul(
                            pS1[:, 0:512], qhp[64:128, qs:qs + 128],
                            khp[64:128, kt * 512:(kt + 1) * 512],
                            start=True, stop=True, tile_position=(64, 0),
                            skip_group_check=True),
                            deps=[prev["r1"]] if prev["r1"] else [])
                        prev["r1"] = s.op("vector", lambda e, kt=kt: e.reduce_max(
                            out=mxc[:, 8 + kt:9 + kt], in_=pS1[:, 0:512], axis=AX.X),
                            deps=[t2])
                    s.op("vector", lambda e: e.reduce_max(
                        out=mxc[:, 16:17], in_=mxc[:, 0:8], axis=AX.X),
                        deps=[prev["r0"], prev["r1"]])
                    s.op("vector", lambda e: e.reduce_max(
                        out=mxc[:, 17:18], in_=mxc[:, 8:16], axis=AX.X))
                    tn = s.op("vector", lambda e: e.tensor_scalar(
                        out=mxc[:, 18:20], in0=mxc[:, 16:18], scalar1=-1.0,
                        scalar2=None, op0=ALU.mult))
                    for hh in range(2):
                        ttr = s.op("tensor", lambda e, hh=hh: e.transpose(
                            pS0[0:1, 0:128] if hh == 0 else pS1[0:1, 0:128],
                            mxc[:, 18 + hh:19 + hh], identf[:]),
                            deps=[tn, prev["r0"], prev["r1"]])
                        tcp = s.op("vector", lambda e, hh=hh, sti=sti: e.tensor_copy(
                            out=negrow[0:1, hh * 512 + sti * 128:
                                       hh * 512 + (sti + 1) * 128],
                            in_=(pS0 if hh == 0 else pS1)[0:1, 0:128]),
                            deps=[ttr])
                        if hh == 0:
                            prev["r0"] = tcp
                        else:
                            prev["r1"] = tcp
                    prev["neg"] = tcp
                for kt in range(NKT):
                    for hh in range(2):
                        pps = pS0 if hh == 0 else pS1
                        pyy = pY0 if hh == 0 else pY1
                        pzz = pZ0 if hh == 0 else pZ1
                        pbb = pbuf0 if hh == 0 else pbuf1
                        d0 = hh * 64
                        deps1 = [prev["neg"]]
                        if kt == 0:
                            deps1 += [d for d in (prev["r0"], prev["r1"]) if d]
                        else:
                            deps1 += [d for d in (prev["e0" if hh == 0 else "e1"],) if d]
                        t1 = s.op("tensor", lambda e, pps=pps, kt=kt, d0=d0, q0=q0:
                                  e.matmul(
                            pps[:, 0:512], khp[d0:d0 + 64, kt * 128:(kt + 1) * 128],
                            qhp[d0:d0 + 64, q0:q0 + 512],
                            start=True, stop=False,
                            tile_position=(d0, 0), skip_group_check=True),
                            deps=deps1)
                        t1b = s.op("tensor", lambda e, pps=pps, hh=hh: e.matmul(
                            pps[:, 0:512], onesr_b[0:1, 0:128],
                            negrow[0:1, hh * 512:(hh + 1) * 512],
                            start=False, stop=True))
                        edeps = [t1b]
                        if kt > 0:
                            edeps += [prev["z0" if hh == 0 else "z1"],
                                      prev["pv0" if hh == 0 else "pv1"]]
                        elif prev["ev"]:
                            edeps += [prev["ev"]]
                        texp = s.op("scalar", lambda e, pps=pps, pbb=pbb: e.activation(
                            out=pbb[:], in_=pps[:, 0:512], func=AF.Exp,
                            bias=0.0, scale=0.125), deps=edeps)
                        prev["e0" if hh == 0 else "e1"] = texp
                        zdeps = [texp]
                        if kt == 0 and prev["ev"]:
                            zdeps += [prev["ev"]]
                        tz = s.op("tensor", lambda e, pzz=pzz, pbb=pbb, kt=kt:
                                  e.matmul(
                            pzz[0:1, 0:512], ones1_b[:, 0:1], pbb[:],
                            start=(kt == 0), stop=(kt == NKT - 1)), deps=zdeps)
                        prev["z0" if hh == 0 else "z1"] = tz
                        tpv = s.op("tensor", lambda e, pyy=pyy, pbb=pbb, kt=kt,
                                   d0=d0: e.matmul(
                            pyy[0:64, 0:512],
                            vhp[:, kt * 128 + d0:kt * 128 + d0 + 64],
                            pbb[:], start=(kt == 0), stop=(kt == NKT - 1)))
                        prev["pv0" if hh == 0 else "pv1"] = tpv
                tzc = s.op("vector", lambda e: e.tensor_copy(
                    out=zrec[0:1, 0:512], in_=pZ0[0:1, 0:512]),
                    deps=[prev["z0"], prev["pv0"], prev["pv1"]])
                tzc2 = s.op("vector", lambda e: e.tensor_copy(
                    out=zrec[0:1, 512:1024], in_=pZ1[0:1, 0:512]),
                    deps=[prev["z1"]])
                trc = s.op("vector", lambda e: e.reciprocal(
                    out=zrec[0:1, 0:1024], in_=zrec[0:1, 0:1024]),
                    deps=[tzc, tzc2])
                tev = None
                for hh in range(2):
                    pyy = pY0 if hh == 0 else pY1
                    pps = pS0 if hh == 0 else pS1
                    tb = s.op("tensor", lambda e, hh=hh, pps=pps: e.matmul(
                        pps[:, 0:512], onesr[0:1, 0:128],
                        zrec[0:1, hh * 512:(hh + 1) * 512], start=True,
                        stop=True), deps=[trc, prev["e0"], prev["e1"]])
                    tcpz = s.op("vector", lambda e, pps=pps: e.tensor_copy(
                        out=zbr[:], in_=pps[:, 0:512]), deps=[tb])
                    tev = s.op("vector", lambda e, pyy=pyy, hh=hh, hp=hp, q0=q0:
                               e.tensor_tensor(
                        out=y_fm[hh * 64:(hh + 1) * 64,
                                 hp * T_FULL + q0:hp * T_FULL + q0 + 512],
                        in0=pyy[0:64, 0:512], in1=zbr[0:64, 0:512],
                        op=ALU.mult), deps=[tcpz])
                    if hh == 0:
                        prev["r0"] = tev
                    else:
                        prev["r1"] = tev
                prev["ev"] = tev
        s.barrier()

        xtmt = sb("xtmt", [128, C], F32)
        outt = sb("outt", [128, C], F32)
        pstb = pt_("pstb", [128, 1024], BF16)
        for ti in range(T_FULL // 128):
            t0 = ti * 128
            tlx = s.dma("sync", xtmt[:], xtm_d[t0:t0 + 128, :])
            ttr = None
            for hp in range(NCH):
                ttr = s.op("tensor", lambda e, hp=hp, t0=t0: e.transpose(
                    pstb[:, hp * 128:(hp + 1) * 128],
                    y_fm[:, hp * T_FULL + t0:hp * T_FULL + t0 + 128], ident[:]))
            tfin = s.op("vector", lambda e: e.tensor_tensor(
                out=outt[:], in0=pstb[:, 0:1024], in1=xtmt[:], op=ALU.add),
                deps=[ttr, tlx])
            s.dma("gpsimd", out_d[t0:t0 + 128, :], outt[:], deps=[tfin])
            s.barrier()
        s.finalize()
    return nc


def run_l3(g, x):
    nc = _get("l3", build_l3)
    base = {
        "wq": np.ascontiguousarray(g["L3_attn_Wq"]),
        "wk": np.ascontiguousarray(g["L3_attn_Wk"]),
        "wv": np.ascontiguousarray(g["L3_attn_Wv"]),
        "lng": g["L3_attn_ln_g"].reshape(C, 1),
        "lnb": g["L3_attn_ln_b"].reshape(C, 1),
        "bq": g["L3_attn_bq"].reshape(C, 1),
        "bk": g["L3_attn_bk"].reshape(C, 1),
        "bv": g["L3_attn_bv"].reshape(C, 1),
        "bvr": g["L3_attn_bv"].reshape(1, C),
        "ident": np.eye(128, dtype=BF),
        "identf": np.eye(128, dtype=np.float32),
    }
    in_maps = []
    for c in range(N_CORES):
        m = dict(base)
        m["x"] = _fm(x[c // 2])
        m["xtm"] = np.ascontiguousarray(x[c // 2])
        in_maps.append(m)
    res = run_bass_kernel_spmd(nc, in_maps, core_ids=list(range(N_CORES)))
    return np.ascontiguousarray(
        np.stack([res.results[2 * b]["xout"] for b in range(B)]))





def build_l1f(T_loc=2048, TG=1024):
    NTG = T_loc // TG
    DF = 4096
    NF = DF // 128
    nc = bass.Bass("TRN2", target_bir_lowering=False)
    x_d = f32in(nc, "x", [C, T_loc])
    w1_d = f32in(nc, "w1T", [C, DF])
    w2_d = f32in(nc, "w2T", [C, DF])
    h1_d = f32in(nc, "h1T", [DF, DF])
    h2_d = f32in(nc, "h2T", [DF, DF])
    w3_d = f32in(nc, "w3T", [DF, C])
    s1_d = f32in(nc, "s1", [DF, 1])
    s2_d = f32in(nc, "s2", [DF, 1])
    sh1_d = f32in(nc, "sh1", [DF, 1])
    sh2_d = f32in(nc, "sh2", [DF, 1])
    s3_d = f32in(nc, "s3", [C, 1])
    out_d = outap(nc, "xout", [C, T_loc])

    with ExitStack() as ctx:
        sb = lambda name, shape, dt: ctx.enter_context(nc.sbuf_tensor(name, shape, dt))
        pt_ = lambda name, shape, dt: ctx.enter_context(nc.psum_tensor(name, shape, dt))
        s = Sched(nc, ctx)

        onesr = sb("onesr", [1, 128], F32)
        ones1_b = sb("ones1_b", [128, 4], BF16)
        epsr = sb("epsr", [1, 2], F32)
        rows = sb("rows", [1, 2 * TG], F32)
        scol = sb("scol", [128, NF * 4 + NCH], F32)
        h_bf = sb("h_bf", [128, NF * TG], BF16)
        g_bf = sb("g_bf", [128, NF * TG], BF16)
        xn_b = sb("xn_b", [128, NCH * TG], BF16)
        wbufA = sb("wbufA", [128, 32 * 128], BF16)
        wbufB = sb("wbufB", [128, 32 * 128], BF16)
        wstg = sb("wstg", [128, 2048], F32)
        xt = sb("xt", [128, NCH * 512], F32)
        x_bf = sb("x_bfT", [128, NCH * 512], BF16)
        sq_bf = x_bf
        sil = sb("sil", [128, TG], BF16)
        xres = sb("xres", [128, 512], F32)

        p0 = pt_("p0", [128, 512], F32)
        p1 = pt_("p1", [128, 512], F32)
        p2 = pt_("p2", [128, 512], F32)
        p3 = pt_("p3", [128, 512], F32)
        pbc = pt_("pbc", [128, 512], F32)
        prb = pt_("prb", [128, 512], F32)
        pm = pt_("pm", [128, 512], F32)
        psq = pt_("psq", [128, 512], F32)
        rot = Rot([p0, p1])
        rot2 = Rot([p2, p3])

        s.op("gpsimd", lambda e: e.memset(onesr[:], 1.0))
        s.op("gpsimd", lambda e: e.memset(ones1_b[:], 1.0))
        s.op("gpsimd", lambda e: e.memset(epsr[:], 1e-5))
        for j in range(NF):
            s.dma("sync", scol[:, j:j + 1], s1_d[j * 128:(j + 1) * 128, :])
            s.dma("sync", scol[:, 32 + j:33 + j], s2_d[j * 128:(j + 1) * 128, :])
            s.dma("sync", scol[:, 64 + j:65 + j], sh1_d[j * 128:(j + 1) * 128, :])
            s.dma("sync", scol[:, 96 + j:97 + j], sh2_d[j * 128:(j + 1) * 128, :])
        for j in range(NCH):
            s.dma("sync", scol[:, 128 + j:129 + j], s3_d[j * 128:(j + 1) * 128, :])
        s.barrier()

        def stream_w(wd, nch_in, cols_slice, buf, deps=()):
            tc = None
            for w0 in range(0, nch_in, 16):
                wlen = min(16, nch_in - w0)
                tl = None
                for ci in range(wlen):
                    c2 = w0 + ci
                    tl = s.dma("sync", wstg[:, ci * 128:(ci + 1) * 128],
                               wd[c2 * 128:(c2 + 1) * 128, cols_slice],
                               deps=(list(deps) if (c2 == 0 and deps) else []) +
                                    ([tc] if (ci == 0 and tc) else []))
                for ci in range(wlen):
                    c2 = w0 + ci
                    tc = s.op("gpsimd", lambda e, c2=c2, ci=ci, buf=buf:
                              e.tensor_copy(
                        out=buf[:, c2 * 128:(c2 + 1) * 128],
                        in_=wstg[:, ci * 128:(ci + 1) * 128]), deps=[tl])
            return tc

        rM, rR = rows[:, 0:TG], rows[:, TG:2 * TG]
        for tg in range(NTG):
            t0 = tg * TG
            for hf in range(TG // 512):
                hsl = slice(hf * 512, (hf + 1) * 512)
                tl = None
                for c2 in range(NCH):
                    tl = s.dma("sync", xt[:, c2 * 512:(c2 + 1) * 512],
                               x_d[c2 * 128:(c2 + 1) * 128,
                                   t0 + hf * 512:t0 + (hf + 1) * 512])
                tc = None
                for c2 in range(NCH):
                    tc = s.op("gpsimd", lambda e, c2=c2: e.tensor_copy(
                        out=x_bf[:, c2 * 512:(c2 + 1) * 512],
                        in_=xt[:, c2 * 512:(c2 + 1) * 512]), deps=[tl])
                tvh = ln_stats(
                    s, x_bf, sq_bf, ones1_b, pm[0:1, 0:512], psq[0:1, 0:512],
                    rM[0:1, hsl], rR[0:1, hsl], 512, tc)
                trs = rstd_row(s, rR[0:1, hsl], rR[0:1, hsl], tvh,
                               epsr[0:1, 0:1])
                tb1 = bcast(s, pbc[:, 0:512], rM[0:1, hsl], onesr, [trs])
                tb2 = bcast(s, prb[:, 0:512], rR[0:1, hsl], onesr, [])
                for c2 in range(NCH):
                    sl2 = slice(c2 * 512, (c2 + 1) * 512)
                    sl = slice(c2 * TG + hf * 512, c2 * TG + (hf + 1) * 512)
                    s.op("vector", lambda e, sl2=sl2: e.tensor_tensor(
                        out=xt[:, sl2], in0=xt[:, sl2], in1=pbc[:, 0:512],
                        op=ALU.subtract), deps=[tb1, tb2])
                    s.op("vector", lambda e, sl=sl, sl2=sl2: e.tensor_tensor(
                        out=xn_b[:, sl], in0=xt[:, sl2], in1=prb[:, 0:512],
                        op=ALU.mult))
                s.barrier()
            twA = twB = None
            prev_th = None
            for j in range(NF):
                twA = stream_w(w1_d, NCH, slice(j * 128, (j + 1) * 128), wbufA,
                               deps=[twA] if twA else ())
                for hf in range(TG // 512):
                    ps, war = rot.acquire()
                    pairs = [(wbufA[:, c2 * 128:(c2 + 1) * 128],
                              xn_b[:, c2 * TG + hf * 512:c2 * TG + (hf + 1) * 512])
                             for c2 in range(NCH)]
                    tmm = mm_group(s, ps[:, 0:512], pairs, war_tok=war,
                                   extra_dep=twA)
                    tsl = s.op("scalar", lambda e, ps=ps, j=j, hf=hf: e.activation(
                        out=sil[:, hf * 512:(hf + 1) * 512], in_=ps[:, 0:512],
                        func=AF.Silu, bias=0.0, scale=scol[:, j:j + 1]),
                        deps=[tmm] + ([prev_th] if prev_th else []))
                    rot.release(tsl)
                twA = tsl
                twB = stream_w(w2_d, NCH, slice(j * 128, (j + 1) * 128), wbufB,
                               deps=[twB] if twB else ())
                for hf in range(TG // 512):
                    ps, war = rot2.acquire()
                    pairs = [(wbufB[:, c2 * 128:(c2 + 1) * 128],
                              xn_b[:, c2 * TG + hf * 512:c2 * TG + (hf + 1) * 512])
                             for c2 in range(NCH)]
                    tmm = mm_group(s, ps[:, 0:512], pairs, war_tok=war,
                                   extra_dep=twB)
                    tmu = s.op("vector", lambda e, ps=ps, j=j, hf=hf: e.tensor_scalar(
                        out=h_bf[:, j * TG + hf * 512:j * TG + (hf + 1) * 512],
                        in0=ps[:, 0:512], scalar1=scol[:, 32 + j:33 + j],
                        scalar2=None, op0=ALU.mult), deps=[tmm])
                    th = s.op("vector", lambda e, j=j, hf=hf: e.tensor_tensor(
                        out=h_bf[:, j * TG + hf * 512:j * TG + (hf + 1) * 512],
                        in0=h_bf[:, j * TG + hf * 512:j * TG + (hf + 1) * 512],
                        in1=sil[:, hf * 512:(hf + 1) * 512], op=ALU.mult),
                        deps=[twA])
                    rot2.release(th)
                    prev_th = th
                twB = th
            s.barrier()
            for (wd, scol0, src_buf, dst_buf) in ((h1_d, 64, h_bf, g_bf),
                                                  (h2_d, 96, g_bf, h_bf)):
                tw = None
                for j in range(NF):
                    tw = stream_w(wd, NF, slice(j * 128, (j + 1) * 128), wbufA,
                                  deps=[tw] if tw else ())
                    for hf in range(TG // 512):
                        ps, war = rot.acquire()
                        pairs = [(wbufA[:, c2 * 128:(c2 + 1) * 128],
                                  src_buf[:, c2 * TG + hf * 512:c2 * TG + (hf + 1) * 512])
                                 for c2 in range(NF)]
                        tmm = mm_group(s, ps[:, 0:512], pairs, war_tok=war,
                                       extra_dep=tw)
                        tg2 = s.op("scalar", lambda e, ps=ps, j=j, hf=hf,
                                   scol0=scol0, dst_buf=dst_buf: e.activation(
                            out=dst_buf[:, j * TG + hf * 512:j * TG + (hf + 1) * 512],
                            in_=ps[:, 0:512], func=AF.Gelu, bias=0.0,
                            scale=scol[:, scol0 + j:scol0 + j + 1]), deps=[tmm])
                        rot.release(tg2)
                    tw = tg2
                s.barrier()
            tw = None
            prev_fin = None
            for j in range(NCH):
                tw = stream_w(w3_d, NF, slice(j * 128, (j + 1) * 128), wbufA,
                              deps=[tw] if tw else ())
                for hf in range(TG // 512):
                    tl2 = s.dma("sync", xres[:, 0:512],
                                x_d[j * 128:(j + 1) * 128,
                                    t0 + hf * 512:t0 + (hf + 1) * 512],
                                deps=[prev_fin] if prev_fin else [])
                    ps, war = rot.acquire()
                    pairs = [(wbufA[:, c2 * 128:(c2 + 1) * 128],
                              h_bf[:, c2 * TG + hf * 512:c2 * TG + (hf + 1) * 512])
                             for c2 in range(NF)]
                    tmm = mm_group(s, ps[:, 0:512], pairs, war_tok=war,
                                   extra_dep=tw)
                    tfin = s.op("vector", lambda e, ps=ps, j=j: e.scalar_tensor_tensor(
                        out=xres[:, 0:512], in0=ps[:, 0:512],
                        scalar=scol[:, 128 + j:129 + j], in1=xres[:, 0:512],
                        op0=ALU.mult, op1=ALU.add), deps=[tmm, tl2])
                    rot.release(tfin)
                    prev_fin = s.dma("gpsimd",
                          out_d[j * 128:(j + 1) * 128, t0 + hf * 512:t0 + (hf + 1) * 512],
                          xres[:, 0:512], deps=[tfin])
                tw = tfin
            s.barrier()
        s.finalize()
    return nc


def run_l1f(g, x):
    nc = _get("l1f", build_l1f)
    xf = x.reshape(B * T_FULL, C)
    base = {
        "w1T": np.ascontiguousarray(g["L1_ffn_w1_w"].T),
        "w2T": np.ascontiguousarray(g["L1_ffn_w2_w"].T),
        "h1T": np.ascontiguousarray(g["L1_ffn_h1_w"].T),
        "h2T": np.ascontiguousarray(g["L1_ffn_h2_w"].T),
        "w3T": np.ascontiguousarray(g["L1_ffn_w3_w"].T),
        "s1": g["L1_ffn_w1_s"].reshape(-1, 1),
        "s2": g["L1_ffn_w2_s"].reshape(-1, 1),
        "sh1": g["L1_ffn_h1_s"].reshape(-1, 1),
        "sh2": g["L1_ffn_h2_s"].reshape(-1, 1),
        "s3": g["L1_ffn_w3_s"].reshape(-1, 1),
    }
    in_maps = []
    for c in range(N_CORES):
        m = dict(base)
        m["x"] = _fm(xf[c * 2048:(c + 1) * 2048])
        in_maps.append(m)
    res = run_bass_kernel_spmd(nc, in_maps, core_ids=list(range(N_CORES)))
    out = np.concatenate([res.results[c]["xout"].T for c in range(N_CORES)], 0)
    return np.ascontiguousarray(out.reshape(B, T_FULL, C))





def build_moe(which, T_loc=2048, TT=512, dbg=False):
    NT = T_loc // TT
    NE = 16
    ED = 512
    NEC = ED // 128
    nc = bass.Bass("TRN2", target_bir_lowering=False)
    x_d = f32in(nc, "x", [C, T_loc])
    rw_d = f32in(nc, "rwT", [C, NE])
    ident_d = bf16in(nc, "ident", [128, 128])
    up_d = f32in(nc, "upT", [NE * C, ED])
    dn_d = f32in(nc, "dnT", [NE * ED, C])
    out_d = outap(nc, "xout", [C, T_loc])
    dbg_d = {}
    if dbg:
        dbg_d["gate"] = outap(nc, "dbg_gate", [128, 64])
        dbg_d["acte"] = outap(nc, "dbg_acte", [128, 2048])
        dbg_d["outf"] = outap(nc, "dbg_outf", [128, 4096])
        dbg_d["xn"] = outap(nc, "dbg_xn", [128, 4096])
        dbg_d["lg"] = outap(nc, "dbg_lg", [128, 128])
    if which == "l2f":
        shu_d = f32in(nc, "shuT", [2 * C, ED])
        shd_d = f32in(nc, "shdT", [2 * ED, C])
    else:
        gp_d = f32in(nc, "gpT", [NE * C, ED])
        rb_d = f32in(nc, "rbias", [1, NE])
        shg_d = f32in(nc, "shgT", [C, C])
        shu2_d = f32in(nc, "shuT2", [C, C])
        shd2_d = f32in(nc, "shdT2", [C, C])

    with ExitStack() as ctx:
        sb = lambda name, shape, dt: ctx.enter_context(nc.sbuf_tensor(name, shape, dt))
        pt_ = lambda name, shape, dt: ctx.enter_context(nc.psum_tensor(name, shape, dt))
        s = Sched(nc, ctx)

        ident = sb("ident_sb", [128, 128], BF16)
        onesr = sb("onesr", [1, 128], F32)
        ones1_b = sb("ones1_b", [128, 4], BF16)
        onesr_b = sb("onesr_b", [1, 128], BF16)
        epsr = sb("epsr", [1, 2], F32)
        rows = sb("rows", [1, 2 * TT], F32)
        rw_b = sb("rw_b", [128, NCH * NE], BF16)
        stage = sb("stage", [128, 2 * C], F32)
        xt = sb("xt", [128, NCH * TT], F32)
        x_bf = sb("x_bfT", [128, NCH * TT], BF16)
        xn_b = sb("xn_b", [128, NCH * TT], BF16)
        gate_f = sb("gate_f", [128, 4 * NE], F32)
        lgs = sb("lgs", [128, 8 * NE], F32)
        upe_b = sb("upe_b", [128, NCH * ED], BF16)
        gpe_b = sb("gpe_b", [128, NCH * ED], BF16) if which == "l3f" else None
        dne_b = sb("dne_b", [128, NCH * C], BF16)
        acte_b = sb("acte_b", [128, NCH * TT], BF16)
        actt_b = sb("actt_b", [128, TT], BF16)
        silt_b = sb("silt_b", [128, TT], BF16)
        out_f = sb("out_f", [128, NCH * TT], F32)
        if which == "l3f":
            rbias_b = sb("rbias_b", [1, NE], BF16)

        p0 = pt_("p0", [128, 512], F32)
        p1 = pt_("p1", [128, 512], F32)
        pup = pt_("pup", [128, 512], F32)
        ptrb = pt_("ptrb", [128, 1024], BF16)
        pbc = pt_("pbc", [128, 512], F32)
        prb = pt_("prb", [128, 512], F32)
        pm = pt_("pm", [128, 512], F32)
        psq = pt_("psq", [128, 512], F32)
        rot = Rot([p0, p1])

        s.dma("sync", ident[:], ident_d[:])
        s.op("gpsimd", lambda e: e.memset(onesr[:], 1.0))
        s.op("gpsimd", lambda e: e.memset(ones1_b[:], 1.0))
        s.op("gpsimd", lambda e: e.memset(onesr_b[:], 1.0))
        s.op("gpsimd", lambda e: e.memset(epsr[:], 1e-5))
        load_cast_weight(s, rw_d, rw_b, stage, NE, NCH)
        if which == "l3f":
            tl = s.dma("sync", stage[0:1, 0:NE], rb_d[0:1, :])
            s.op("gpsimd", lambda e: e.tensor_copy(out=rbias_b[0:1, :],
                 in_=stage[0:1, 0:NE]), deps=[tl])
        s.barrier()

        experts = []
        for ei in range(NE):
            experts.append(dict(
                up=up_d[ei * C:(ei + 1) * C, :], upw=ED,
                gp=(gp_d[ei * C:(ei + 1) * C, :] if which == "l3f" else None),
                dn=dn_d[ei * ED:(ei + 1) * ED, :], nec=NEC, gate=ei))
        if which == "l2f":
            for si in range(2):
                experts.append(dict(
                    up=shu_d[si * C:(si + 1) * C, :], upw=ED, gp=None,
                    dn=shd_d[si * ED:(si + 1) * ED, :], nec=NEC, gate=None))
        else:
            for si in range(2):
                experts.append(dict(
                    up=None, upw=ED, gp=None, sh_half=si,
                    dn=None, nec=NEC, gate=None))

        rM, rR = rows[:, 0:TT], rows[:, TT:2 * TT]
        for it in range(NT):
            t0 = it * TT
            tl = None
            for c2 in range(NCH):
                tl = s.dma("sync", xt[:, c2 * TT:(c2 + 1) * TT],
                           x_d[c2 * 128:(c2 + 1) * 128, t0:t0 + TT])
            tc = None
            for c2 in range(NCH):
                tc = s.op("gpsimd", lambda e, c2=c2: e.tensor_copy(
                    out=x_bf[:, c2 * TT:(c2 + 1) * TT],
                    in_=xt[:, c2 * TT:(c2 + 1) * TT]), deps=[tl])
            tv = ln_stats(s, x_bf, x_bf, ones1_b, pm[0:1, 0:TT],
                          psq[0:1, 0:TT], rM, rR, TT, tc)
            trs = rstd_row(s, rR, rR, tv, epsr[0:1, 0:1])
            tb1 = bcast(s, pbc[:, 0:TT], rM[0:1, 0:TT], onesr, [trs])
            tb2 = bcast(s, prb[:, 0:TT], rR[0:1, 0:TT], onesr, [])
            for c2 in range(NCH):
                sl = slice(c2 * TT, (c2 + 1) * TT)
                s.op("vector", lambda e, sl=sl: e.tensor_tensor(
                    out=xt[:, sl], in0=xt[:, sl], in1=pbc[:, 0:TT],
                    op=ALU.subtract), deps=[tb1, tb2])
                s.op("vector", lambda e, sl=sl: e.tensor_tensor(
                    out=xn_b[:, sl], in0=xt[:, sl], in1=prb[:, 0:TT],
                    op=ALU.mult))
            s.op("vector", lambda e: e.memset(out_f[:], 0.0))
            s.barrier()

            L = lgs[:, 0:NE]
            W = lgs[:, NE:2 * NE]
            M1 = lgs[:, 2 * NE:2 * NE + 1]
            M2 = lgs[:, 2 * NE + 1:2 * NE + 2]
            ZZ = lgs[:, 2 * NE + 2:2 * NE + 3]
            ZR = lgs[:, 2 * NE + 3:2 * NE + 4]
            AA = lgs[:, 3 * NE:4 * NE]
            BB = lgs[:, 4 * NE:5 * NE]
            D = lgs[:, 5 * NE:5 * NE + 1]
            W1 = lgs[:, 5 * NE + 1:5 * NE + 2]
            W2 = lgs[:, 5 * NE + 2:5 * NE + 3]
            rtr_war = [None]
            for tsub in range(TT // 128):
                pairs = [(xn_b[:, c2 * TT + tsub * 128:c2 * TT + (tsub + 1) * 128],
                          rw_b[:, c2 * NE:(c2 + 1) * NE]) for c2 in range(NCH)]
                if which == "l3f":
                    tmm0 = mm_group(s, pup[:, 0:NE], pairs, war_tok=rtr_war[0])
                    tmm = s.op("tensor", lambda e: e.matmul(
                        pup[:, 0:NE], onesr_b[0:1, 0:128], rbias_b[0:1, 0:NE],
                        start=False, stop=True, skip_group_check=True),
                        deps=[tmm0])
                else:
                    tmm = mm_group(s, pup[:, 0:NE], pairs, war_tok=rtr_war[0])
                tcp = s.op("vector", lambda e: e.tensor_copy(out=L, in_=pup[:, 0:NE]),
                           deps=[tmm])
                rtr_war[0] = tcp
                if dbg and it == 0 and tsub == 0:
                    s.op("vector", lambda e: e.tensor_copy(
                        out=lgs[:, 96:112], in_=L))
                if which == "l2f":
                    s.op("vector", lambda e: e.reduce_max(out=M1, in_=L, axis=AX.X))
                    s.op("vector", lambda e: e.tensor_scalar(
                        out=M2, in0=M1, scalar1=-1.0, scalar2=None, op0=ALU.mult))
                    tm2 = s.op("vector", lambda e: e.engine_nop() if False else e.tensor_copy(out=W, in_=L))
                    tex = s.op("scalar", lambda e: e.activation(
                        out=L, in_=L, func=AF.Exp, bias=M2, scale=1.0,
                        accum_out=ZZ), deps=[tm2])
                    s.op("vector", lambda e: e.reciprocal(out=ZR, in_=ZZ), deps=[tex])
                    s.op("vector", lambda e: e.tensor_scalar(
                        out=L, in0=L, scalar1=ZR, scalar2=None, op0=ALU.mult))
                    if dbg and it == 0 and tsub == 0:
                        tdum = s.op("vector", lambda e: e.tensor_copy(
                            out=lgs[:, 112:128], in_=L))
                    s.op("vector", lambda e: e.tensor_copy(out=W, in_=L))
                    for rr in range(4):
                        s.op("vector", lambda e: e.reduce_max(out=M1, in_=W, axis=AX.X))
                        if rr < 3:
                            s.op("vector", lambda e: e.tensor_scalar(
                                out=AA, in0=W, scalar1=M1, scalar2=None,
                                op0=ALU.is_ge))
                            s.op("vector", lambda e: e.scalar_tensor_tensor(
                                out=W, in0=AA, scalar=-1e9, in1=W,
                                op0=ALU.mult, op1=ALU.add))
                    s.op("vector", lambda e: e.tensor_scalar(
                        out=AA, in0=L, scalar1=M1, scalar2=None, op0=ALU.is_ge))
                    s.op("vector", lambda e, tsub=tsub: e.tensor_tensor(
                        out=gate_f[:, tsub * NE:(tsub + 1) * NE], in0=L, in1=AA,
                        op=ALU.mult))
                else:
                    s.op("vector", lambda e: e.reduce_max(out=M1, in_=L, axis=AX.X),
                         deps=[tcp])
                    s.op("vector", lambda e: e.tensor_scalar(
                        out=AA, in0=L, scalar1=M1, scalar2=None, op0=ALU.is_ge))
                    s.op("vector", lambda e: e.scalar_tensor_tensor(
                        out=W, in0=AA, scalar=-1e9, in1=L, op0=ALU.mult,
                        op1=ALU.add))
                    s.op("vector", lambda e: e.reduce_max(out=M2, in_=W, axis=AX.X))
                    s.op("vector", lambda e: e.tensor_scalar(
                        out=BB, in0=L, scalar1=M2, scalar2=None, op0=ALU.is_ge))
                    tdd = s.op("vector", lambda e: e.tensor_tensor(
                        out=D, in0=M2, in1=M1, op=ALU.subtract))
                    tex = s.op("scalar", lambda e: e.activation(
                        out=D, in_=D, func=AF.Exp), deps=[tdd])
                    s.op("vector", lambda e: e.tensor_scalar(
                        out=W1, in0=D, scalar1=1.0, scalar2=None, op0=ALU.add),
                        deps=[tex])
                    s.op("vector", lambda e: e.reciprocal(out=W1, in_=W1))
                    s.op("vector", lambda e: e.tensor_tensor(
                        out=W2, in0=D, in1=W1, op=ALU.mult))
                    s.op("vector", lambda e: e.tensor_tensor(
                        out=BB, in0=BB, in1=AA, op=ALU.subtract))
                    s.op("vector", lambda e: e.tensor_scalar(
                        out=AA, in0=AA, scalar1=W1, scalar2=None, op0=ALU.mult))
                    s.op("vector", lambda e: e.tensor_scalar(
                        out=BB, in0=BB, scalar1=W2, scalar2=None, op0=ALU.mult))
                    s.op("vector", lambda e, tsub=tsub: e.tensor_tensor(
                        out=gate_f[:, tsub * NE:(tsub + 1) * NE], in0=AA, in1=BB,
                        op=ALU.add))
            s.barrier()

            pup_war = [None]
            pm_war = [None]
            prev_tcpa = [None]
            for ex in experts:
                nec = ex["nec"]
                sh3 = which == "l3f" and ex.get("sh_half") is not None
                if sh3:
                    hh = ex["sh_half"]
                    upap = shu2_d[:, hh * 512:(hh + 1) * 512]
                    gpap = shg_d[:, hh * 512:(hh + 1) * 512]
                    dnap = shd2_d[hh * 512:(hh + 1) * 512, :]
                else:
                    upap, gpap, dnap = ex["up"], ex["gp"], ex["dn"]
                load_cast_weight(s, upap, upe_b, stage, ED, NCH, stride=ED)
                if gpap is not None:
                    load_cast_weight(s, gpap, gpe_b, stage, ED, NCH, stride=ED)
                load_cast_weight(s, dnap, dne_b, stage, C, nec)
                prev_ttr = [None]
                prev_tact = [None]
                for tsub in range(TT // 128):
                    pairs = [(xn_b[:, c2 * TT + tsub * 128:c2 * TT + (tsub + 1) * 128],
                              upe_b[:, c2 * ED:(c2 + 1) * ED])
                             for c2 in range(NCH)]
                    tup = mm_group(s, pup[:, 0:ED], pairs, war_tok=pup_war[0])
                    if which == "l2f":
                        tact = s.op("scalar", lambda e: e.activation(
                            out=actt_b[:, 0:ED], in_=pup[:, 0:ED], func=AF.Gelu),
                            deps=[tup] + ([prev_ttr[0]] if prev_ttr[0] else []))
                        pup_war[0] = tact
                        if ex["gate"] is not None:
                            gcol = gate_f[:, tsub * NE + ex["gate"]:
                                          tsub * NE + ex["gate"] + 1]
                            tact = s.op("vector", lambda e, gcol=gcol: e.tensor_scalar(
                                out=actt_b[:, 0:ED], in0=actt_b[:, 0:ED],
                                scalar1=gcol, scalar2=None, op0=ALU.mult),
                                deps=[tact])
                    else:
                        pairs2 = [(xn_b[:, c2 * TT + tsub * 128:c2 * TT + (tsub + 1) * 128],
                                   (gpe_b if not sh3 else gpe_b)[:, c2 * ED:(c2 + 1) * ED])
                                  for c2 in range(NCH)]
                        tgp = mm_group(s, pm[:, 0:ED], pairs2, war_tok=pm_war[0])
                        tsl = s.op("scalar", lambda e: e.activation(
                            out=silt_b[:, 0:ED], in_=pm[:, 0:ED], func=AF.Silu),
                            deps=[tgp] + ([prev_tact[0]] if prev_tact[0] else []))
                        pm_war[0] = tsl
                        wdeps = [tup] + ([prev_ttr[0]] if prev_ttr[0] else [])
                        if ex["gate"] is not None:
                            gcol = gate_f[:, tsub * NE + ex["gate"]:
                                          tsub * NE + ex["gate"] + 1]
                            tmu = s.op("vector", lambda e, gcol=gcol: e.tensor_scalar(
                                out=actt_b[:, 0:ED], in0=pup[:, 0:ED],
                                scalar1=gcol, scalar2=None, op0=ALU.mult),
                                deps=wdeps)
                        else:
                            tmu = s.op("vector", lambda e: e.tensor_copy(
                                out=actt_b[:, 0:ED], in_=pup[:, 0:ED]),
                                deps=wdeps)
                        tact = s.op("vector", lambda e: e.tensor_tensor(
                            out=actt_b[:, 0:ED], in0=actt_b[:, 0:ED],
                            in1=silt_b[:, 0:ED], op=ALU.mult), deps=[tsl, tmu])
                        pup_war[0] = tmu
                        prev_tact[0] = tact
                    for ech in range(nec):
                        ttr = s.op("tensor", lambda e, ech=ech: e.transpose(
                            ptrb[:, ech * 128:(ech + 1) * 128],
                            actt_b[:, ech * 128:(ech + 1) * 128], ident[:]),
                            deps=[tact] + ([prev_tcpa[0]] if prev_tcpa[0]
                                           else []))
                        tcpa = s.op("vector", lambda e, ech=ech, tsub=tsub:
                                    e.tensor_copy(
                            out=acte_b[:, ech * TT + tsub * 128:
                                       ech * TT + (tsub + 1) * 128],
                            in_=ptrb[:, ech * 128:(ech + 1) * 128]), deps=[ttr])
                        prev_tcpa[0] = tcpa
                        prev_ttr[0] = ttr

                for j in range(NCH):
                    ps, war = rot.acquire()
                    pairs = [(dne_b[:, ech * C + j * 128:ech * C + (j + 1) * 128],
                              acte_b[:, ech * TT:(ech + 1) * TT])
                             for ech in range(nec)]
                    tmm = mm_group(s, ps[:, 0:TT], pairs, war_tok=war,
                                   extra_dep=tcpa)
                    tadd = s.op("vector", lambda e, j=j, ps=ps: e.tensor_tensor(
                        out=out_f[:, j * TT:(j + 1) * TT],
                        in0=out_f[:, j * TT:(j + 1) * TT], in1=ps[:, 0:TT],
                        op=ALU.add), deps=[tmm])
                    rot.release(tadd)
                s.barrier()
            if dbg and it == 0:
                s.barrier()
                s.dma("gpsimd", dbg_d["gate"], gate_f[:, 0:64])
                s.dma("gpsimd", dbg_d["lg"], lgs[:, 0:128])
                tca = s.op("vector", lambda e: e.tensor_copy(
                    out=out_f[:, 0:0] if False else xt[:, 0:2048],
                    in_=acte_b[:, 0:2048]))
                s.dma("gpsimd", dbg_d["acte"], xt[:, 0:2048], deps=[tca])
                s.dma("gpsimd", dbg_d["outf"], out_f[:, 0:4096])
                tcx = s.op("vector", lambda e: e.tensor_copy(
                    out=xt[:, 2048:4096], in_=xn_b[:, 0:2048]))
                s.dma("gpsimd", dbg_d["xn"], xt[:, 0:4096], deps=[tcx])
                s.barrier()
            tl2 = None
            for c2 in range(NCH):
                tl2 = s.dma("sync", xt[:, c2 * TT:(c2 + 1) * TT],
                            x_d[c2 * 128:(c2 + 1) * 128, t0:t0 + TT])
            for c2 in range(NCH):
                sl = slice(c2 * TT, (c2 + 1) * TT)
                tfin = s.op("vector", lambda e, sl=sl: e.tensor_tensor(
                    out=out_f[:, sl], in0=out_f[:, sl], in1=xt[:, sl],
                    op=ALU.add), deps=[tl2])
                s.dma("gpsimd", out_d[c2 * 128:(c2 + 1) * 128, t0:t0 + TT],
                      out_f[:, sl], deps=[tfin])
            s.barrier()
        s.finalize()
    return nc


def run_l2f(g, x):
    nc = _get("l2f", lambda: build_moe("l2f"))
    xf = x.reshape(B * T_FULL, C)
    base = {
        "rwT": np.ascontiguousarray(g["L2_ffn_router_w"].T),
        "ident": np.eye(128, dtype=BF),
        "upT": np.ascontiguousarray(
            g["L2_ffn_routed_up"].transpose(0, 2, 1).reshape(16 * C, 512)),
        "dnT": np.ascontiguousarray(
            g["L2_ffn_routed_down"].transpose(0, 2, 1).reshape(16 * 512, C)),
        "shuT": np.ascontiguousarray(
            g["L2_ffn_shared_up"].transpose(0, 2, 1).reshape(2 * C, 512)),
        "shdT": np.ascontiguousarray(
            g["L2_ffn_shared_down"].transpose(0, 2, 1).reshape(2 * 512, C)),
    }
    in_maps = []
    for c in range(N_CORES):
        m = dict(base)
        m["x"] = _fm(xf[c * 2048:(c + 1) * 2048])
        in_maps.append(m)
    res = run_bass_kernel_spmd(nc, in_maps, core_ids=list(range(N_CORES)))
    out = np.concatenate([res.results[c]["xout"].T for c in range(N_CORES)], 0)
    return np.ascontiguousarray(out.reshape(B, T_FULL, C))


def run_l3f(g, x):
    nc = _get("l3f", lambda: build_moe("l3f"))
    xf = x.reshape(B * T_FULL, C)
    base = {
        "rwT": np.ascontiguousarray(g["L3_ffn_router_w"].T),
        "rbias": g["L3_ffn_router_bias"].reshape(1, 16),
        "ident": np.eye(128, dtype=BF),
        "upT": np.ascontiguousarray(g["L3_ffn_up_proj"].reshape(16 * C, 512)),
        "gpT": np.ascontiguousarray(g["L3_ffn_gate_proj"].reshape(16 * C, 512)),
        "dnT": np.ascontiguousarray(g["L3_ffn_down_proj"].reshape(16 * 512, C)),
        "shgT": np.ascontiguousarray(g["L3_ffn_sh_gate"].T),
        "shuT2": np.ascontiguousarray(g["L3_ffn_sh_up"].T),
        "shdT2": np.ascontiguousarray(g["L3_ffn_sh_down"].T),
    }
    in_maps = []
    for c in range(N_CORES):
        m = dict(base)
        m["x"] = _fm(xf[c * 2048:(c + 1) * 2048])
        in_maps.append(m)
    res = run_bass_kernel_spmd(nc, in_maps, core_ids=list(range(N_CORES)))
    out = np.concatenate([res.results[c]["xout"].T for c in range(N_CORES)], 0)
    return np.ascontiguousarray(out.reshape(B, T_FULL, C))





def build_l1(TT=512, dbg=False):
    NT = T_FULL // TT
    NH = 8
    nc = bass.Bass("TRN2", target_bir_lowering=False)
    x_d = f32in(nc, "x", [C, T_FULL])
    wq_d = f32in(nc, "wqT", [C, C])
    wk_d = f32in(nc, "wkT", [C, C])
    wv_d = f32in(nc, "wvT", [C, C])
    wo_d = f32in(nc, "woT", [C, C])
    lnw_d = f32in(nc, "lnw", [C, 1])
    lnb_d = f32in(nc, "lnb", [C, 1])
    a_d = f32in(nc, "ah", [NH * 128, 128])
    at_d = f32in(nc, "ahT", [NH * 128, 128])
    knull_d = f32in(nc, "knullT", [128, NH])
    cos_d = bf16in(nc, "cosb", [64, T_FULL])
    sin_d = bf16in(nc, "sinb", [64, T_FULL])
    efm_d = bf16in(nc, "efm", [128, T_FULL])
    etm_d = bf16in(nc, "etm", [T_FULL, 128])
    wmask_d = bf16in(nc, "wmask", [128, 5 * 512])
    wbias_d = bf16in(nc, "wbias", [128, 5 * 512])
    nmask_d = bf16in(nc, "nmask", [128, 2 * 128])
    nbias_d = bf16in(nc, "nbias", [128, 2 * 128])
    nrow_d = f32in(nc, "nrow", [1, T_FULL])
    ident_d = bf16in(nc, "ident", [128, 128])
    out_d = outap(nc, "xout", [C, T_FULL])
    y_scr = nc.dram_tensor("y_scr", [C, T_FULL], BF16, kind="Internal").ap()
    if dbg:
        dbgy_d = outap(nc, "dbg_y", [C, T_FULL], BF16)
        dbgq_d = outap(nc, "dbg_q", [128, T_FULL], BF16)
        dbgk_d = outap(nc, "dbg_k", [128, T_FULL], BF16)
        dbgv_d = outap(nc, "dbg_v", [128, T_FULL], BF16)
        dbgu_d = outap(nc, "dbg_u", [128, T_FULL], BF16)
        dbgxv_d = outap(nc, "dbg_xv", [128, T_FULL], BF16)
        dbgg_d = outap(nc, "dbg_g1", [128, 512], BF16)
        dbgv2_d = outap(nc, "dbg_v2", [128, T_FULL], BF16)
        dbgrv_d = outap(nc, "dbg_rvp", [128, 32], F32)
        dbgbv_d = outap(nc, "dbg_bv", [128, 1024], BF16)
        dbgall_d = outap(nc, "dbg_all", [128, 8 * 2688], BF16)
        dbgqf_d = outap(nc, "dbg_qf", [128, 128], BF16)
        dbgkf_d = outap(nc, "dbg_kf", [128, 128], BF16)
        dbgvn_d = outap(nc, "dbg_vn", [128, 128], BF16)
        dbguf_d = outap(nc, "dbg_uf", [128, 128], BF16)
        dbgfc_d = outap(nc, "dbg_fc", [128, 128], BF16)
        dbgwk2_d = outap(nc, "dbg_wk2", [128, 8 * 8 * 128], BF16)
        dbgwv2_d = outap(nc, "dbg_wv2", [128, 8 * 8 * 128], BF16)
        dbgvfull_d = outap(nc, "dbg_vfull", [128, 8 * T_FULL], BF16)
        dbgkfull_d = outap(nc, "dbg_kfull", [128, 8 * T_FULL], BF16)
        dbgxin7a_d = outap(nc, "dbg_xin7a", [128, NCH * T_FULL], BF16)
        dbgxin7b_d = outap(nc, "dbg_xin7b", [128, NCH * T_FULL], BF16)

    with ExitStack() as ctx:
        sb = lambda name, shape, dt: ctx.enter_context(nc.sbuf_tensor(name, shape, dt))
        pt_ = lambda name, shape, dt: ctx.enter_context(nc.psum_tensor(name, shape, dt))
        s = Sched(nc, ctx)

        ident = sb("ident_sb", [128, 128], BF16)
        onesr = sb("onesr", [1, 128], F32)
        ones1_b = sb("ones1_b", [128, 4], BF16)
        onesr_b = sb("onesr_b", [1, 4], BF16)
        onesc_f = sb("onesc_f", [128, 1], F32)
        epsr = sb("epsr", [1, 4], F32)
        rows = sb("rows", [1, 4 * TT], F32)
        cosb = sb("cosb_sb", [64, T_FULL], BF16)
        sinb = sb("sinb_sb", [64, T_FULL], BF16)
        efm = sb("efm_sb", [128, T_FULL], BF16)
        etm = sb("etm_sb", [128, 32 * 128], BF16)
        wmask = sb("wmask_sb", [128, 5 * 512], BF16)
        wbias = sb("wbias_sb", [128, 5 * 512], BF16)
        nmask = sb("nmask_sb", [128, 2 * 128], BF16)
        nbias = sb("nbias_sb", [128, 2 * 128], BF16)
        nrowc = sb("nrow_sb", [1, 512], F32)
        knt = sb("knt", [128, NH], F32)
        lncol = sb("lncol", [128, 16], F32)
        lnb_bf = sb("lnb_bf", [128, NCH], BF16)
        rvp_cols = sb("rvp_cols", [128, 32], F32)
        biasvb = sb("biasvb", [128, C], BF16)
        brow = sb("brow", [1, C], F32)
        xin_b = sb("xin_b", [128, NCH * T_FULL], BF16)
        wstg_b = sb("wstg_b", [128, 512], BF16)
        stage = sb("stage", [128, 2 * C], F32)


        pS0 = pt_("pS0", [128, 512], F32)
        pS1 = pt_("pS1", [128, 512], F32)
        pA = pt_("pA", [128, 512], F32)
        pB = pt_("pB", [128, 512], F32)
        pC = pt_("pC", [128, 512], F32)
        pY = pt_("pY", [128, 512], F32)
        pZr = pt_("pZr", [128, 512], F32)
        pT = pt_("pT", [128, 1024], BF16)
        srot = Rot([pS0, pS1])

        s.dma("sync", ident[:], ident_d[:])
        s.dma("sync", cosb[:], cos_d[:])
        s.dma("sync", sinb[:], sin_d[:])
        s.dma("sync", efm[:], efm_d[:])
        for ti in range(32):
            s.dma("sync", etm[:, ti * 128:(ti + 1) * 128],
                  etm_d[ti * 128:(ti + 1) * 128, :])
        s.dma("sync", wmask[:], wmask_d[:])
        s.dma("sync", wbias[:], wbias_d[:])
        s.dma("sync", nmask[:], nmask_d[:])
        s.dma("sync", nbias[:], nbias_d[:])
        s.dma("sync", knt[:], knull_d[:])
        tlncol = None
        for c2 in range(NCH):
            s.dma("sync", lncol[:, c2:c2 + 1], lnw_d[c2 * 128:(c2 + 1) * 128, :])
            tlncol = s.dma("sync", lncol[:, 8 + c2:9 + c2],
                           lnb_d[c2 * 128:(c2 + 1) * 128, :])
        s.op("gpsimd", lambda e: e.memset(onesr[:], 1.0))
        s.op("gpsimd", lambda e: e.memset(ones1_b[:], 1.0))
        s.op("gpsimd", lambda e: e.memset(onesr_b[:], 1.0))
        s.op("gpsimd", lambda e: e.memset(onesc_f[:], 1.0))
        s.op("gpsimd", lambda e: e.memset(epsr[:, 0:1], 1e-5))
        s.op("gpsimd", lambda e: e.memset(epsr[:, 1:2], 1e-6))
        s.op("gpsimd", lambda e: e.memset(epsr[:, 2:3], 1e-12))
        s.op("vector", lambda e: e.tensor_copy(out=lnb_bf[:], in_=lncol[:, 8:16]),
             deps=[tlncol])
        s.barrier()
        for half in range(2):
            for c2 in range(NCH):
                tl = s.dma("sync", stage[:, 1024:1536],
                           wv_d[c2 * 128:(c2 + 1) * 128,
                                half * 512:(half + 1) * 512])
                tc = s.op("gpsimd", lambda e: e.tensor_copy(
                    out=wstg_b[:], in_=stage[:, 1024:1536]), deps=[tl])
                tmm = s.op("tensor", lambda e, c2=c2: e.matmul(
                    pC[0:1, 0:512], lnb_bf[:, c2:c2 + 1], wstg_b[:],
                    start=(c2 == 0), stop=(c2 == NCH - 1)), deps=[tc])
                s.wait("gpsimd", tmm)
                s.wait("sync", tmm)
            tev = s.op("vector", lambda e, half=half: e.tensor_copy(
                out=brow[0:1, half * 512:(half + 1) * 512],
                in_=pC[0:1, 0:512]), deps=[tmm])
            s.wait("tensor", tev)
        for half in range(2):
            tb = s.op("tensor", lambda e, half=half: e.matmul(
                pC[:, 0:512], onesr[0:1, 0:128],
                brow[0:1, half * 512:(half + 1) * 512], start=True, stop=True))
            tcv = s.op("vector", lambda e, half=half: e.tensor_copy(
                out=biasvb[:, half * 512:(half + 1) * 512], in_=pC[:, 0:512]),
                deps=[tb])
            s.wait("tensor", tcv)
        s.barrier()

        actx = ExitStack()
        xt = actx.enter_context(nc.sbuf_tensor("xtA", [128, NCH * TT], F32))
        x_bf = actx.enter_context(nc.sbuf_tensor("x_bfA", [128, NCH * TT], BF16))
        rows = actx.enter_context(nc.sbuf_tensor("rowsA", [1, 4 * TT], F32))
        R = lambda i: rows[:, i * TT:(i + 1) * TT]
        rM, rR, rT2, rVP = R(0), R(1), R(2), R(3)
        for it in range(NT):
            t0 = it * TT
            tl = None
            for c2 in range(NCH):
                tl = s.dma("sync", xt[:, c2 * TT:(c2 + 1) * TT],
                           x_d[c2 * 128:(c2 + 1) * 128, t0:t0 + TT])
            tc = None
            for c2 in range(NCH):
                tc = s.op("gpsimd", lambda e, c2=c2: e.tensor_copy(
                    out=x_bf[:, c2 * TT:(c2 + 1) * TT],
                    in_=xt[:, c2 * TT:(c2 + 1) * TT]), deps=[tl])
            tv = ln_stats(s, x_bf, x_bf, ones1_b, pA[0:1, 0:TT],
                          pB[0:1, 0:TT], rM, rR, TT, tc)
            s.op("vector", lambda e: e.tensor_copy(
                out=rT2[0:1, 0:TT], in_=rR[0:1, 0:TT]), deps=[tv])
            tr1 = rstd_row(s, rR, rR, tv, epsr[0:1, 0:1])
            s.op("vector", lambda e: e.tensor_tensor(
                out=rVP[0:1, 0:TT], in0=rR[0:1, 0:TT], in1=rR[0:1, 0:TT],
                op=ALU.mult), deps=[tr1])
            tvv = s.op("vector", lambda e: e.tensor_tensor(
                out=rT2[0:1, 0:TT], in0=rT2[0:1, 0:TT], in1=rVP[0:1, 0:TT],
                op=ALU.mult))
            tsq2 = s.op("scalar", lambda e: e.activation(
                out=rVP[0:1, 0:TT], in_=rT2[0:1, 0:TT], func=AF.Sqrt,
                bias=epsr[0:1, 0:1], scale=1.0), deps=[tvv])
            trcp = s.op("vector", lambda e: e.reciprocal(
                out=rVP[0:1, 0:TT], in_=rVP[0:1, 0:TT]), deps=[tsq2])
            for tsub in range(TT // 128):
                tcol = s.op("tensor", lambda e, tsub=tsub: e.matmul(
                    pC[:, 0:1], rVP[0:1, tsub * 128:(tsub + 1) * 128],
                    onesr[0:1, 0:1], start=True, stop=True), deps=[trcp])
                tcc = s.op("vector", lambda e, it=it, tsub=tsub: e.tensor_copy(
                    out=rvp_cols[:, it * 4 + tsub:it * 4 + tsub + 1],
                    in_=pC[:, 0:1]), deps=[tcol])
                s.wait("tensor", tcc)
            tb1 = bcast(s, pA[:, 0:TT], rM[0:1, 0:TT], onesr, [trcp])
            tb2 = bcast(s, pB[:, 0:TT], rR[0:1, 0:TT], onesr, [])
            for c2 in range(NCH):
                sl = slice(c2 * TT, (c2 + 1) * TT)
                s.op("vector", lambda e, sl=sl: e.tensor_tensor(
                    out=xt[:, sl], in0=xt[:, sl], in1=pA[:, 0:TT],
                    op=ALU.subtract), deps=[tb1, tb2])
                s.op("vector", lambda e, sl=sl, t0=t0, c2=c2: e.tensor_tensor(
                    out=xin_b[:, c2 * T_FULL + t0:c2 * T_FULL + t0 + TT],
                    in0=xt[:, sl], in1=pB[:, 0:TT], op=ALU.mult))
            s.barrier()
        actx.close()

        bctx = ExitStack()
        sbB = lambda name, shape, dt: bctx.enter_context(nc.sbuf_tensor(name, shape, dt))
        sb = sbB
        q_fm = sb("q_fm", [128, T_FULL], BF16)
        k_fm = sb("k_fm", [128, T_FULL], BF16)
        xv_b = sb("xv_b", [128, T_FULL], BF16)
        un_b = sb("un_b", [128, T_FULL], BF16)
        v_b = sb("v_b", [128, T_FULL], BF16)
        wqh = sb("wqh", [128, NCH * 128], BF16)
        wkh = sb("wkh", [128, NCH * 128], BF16)
        wvh = sb("wvh", [128, NCH * 128], BF16)
        smb = sb("smb", [128, 12 * 128 + 512], BF16)
        sh_b = smb[:, 0:128]
        kn_b = smb[:, 128:129]
        qf_b = smb[:, 256:384]
        kf_b = smb[:, 384:512]
        vn_b = smb[:, 512:640]
        qfT_b = smb[:, 640:768]
        kfT_b = smb[:, 768:896]
        vnT_b = smb[:, 896:1024]
        ufT_b = smb[:, 1024:1152]
        fc_b = smb[:, 1152:1280]
        trs_b = smb[:, 1280:1408]
        g1_b = smb[:, 1536:2048]
        bigb = sb("bigb", [128, 7 * 512], BF16)
        sk_b = bigb[:, 0:1024]
        sksq_b = bigb[:, 1024:2048]
        p_bf = bigb[:, 2048:3072]
        yev = bigb[:, 3072:3584]
        bigf = sb("bigf", [128, 3 * 512], F32)
        zb_f = bigf[:, 0:512]
        rw3 = zb_f[0:64, :]
        mub_sb = bigf[:, 512:1024]
        rbb_sb = bigf[:, 1024:1536]
        zrb_sb = mub_sb
        rw1 = bigf[0:64, 512:1024]
        rw2 = bigf[0:64, 1024:1536]
        rowf = sb("rowf", [1, 1024 + 1024], F32)
        snull_r = rowf[0:1, 0:1024]
        srow = rowf[0:1, 1024:2048]

        def fm_norm(dst_b, src_getter, nt_tiles, eps_ap, maxeps, rowbuf,
                    extra_mul=None):
            pass

        for h in range(NH):
            load_cast_weight(s, wq_d[:, h * 128:(h + 1) * 128], wqh, stage,
                             128, NCH)
            load_cast_weight(s, wk_d[:, h * 128:(h + 1) * 128], wkh, stage,
                             128, NCH)
            load_cast_weight(s, wv_d[:, h * 128:(h + 1) * 128], wvh, stage,
                             128, NCH)
            for c2 in range(NCH):
                s.op("vector", lambda e, c2=c2: e.tensor_scalar(
                    out=wvh[:, c2 * 128:(c2 + 1) * 128],
                    in0=wvh[:, c2 * 128:(c2 + 1) * 128],
                    scalar1=lncol[:, c2:c2 + 1], scalar2=None, op0=ALU.mult))
            s.barrier()
            tl1 = s.dma("sync", stage[:, 0:128], a_d[h * 128:(h + 1) * 128, :])
            tl2 = s.dma("sync", stage[:, 128:256], at_d[h * 128:(h + 1) * 128, :])
            tsh = s.op("vector", lambda e: e.tensor_tensor(
                out=stage[:, 256:384], in0=stage[:, 0:128], in1=stage[:, 128:256],
                op=ALU.subtract), deps=[tl1, tl2])
            tsh = s.op("vector", lambda e: e.tensor_scalar(
                out=sh_b[:], in0=stage[:, 256:384], scalar1=0.5, scalar2=None,
                op0=ALU.mult), deps=[tsh])
            tnq = s.op("scalar", lambda e, h=h: e.activation(
                out=stage[:, 384:385], in_=knt[:, h:h + 1], func=AF.Square))
            tns = s.op("tensor", lambda e: e.matmul(
                pC[0:1, 0:1], stage[:, 384:385], onesc_f[:], start=True,
                stop=True), deps=[tnq])
            tnr = s.op("vector", lambda e: e.tensor_copy(
                out=srow[0:1, 0:1], in_=pC[0:1, 0:1]), deps=[tns])
            tns2 = s.op("scalar", lambda e: e.activation(
                out=srow[0:1, 0:1], in_=srow[0:1, 0:1], func=AF.Sqrt,
                bias=0.0, scale=1.0), deps=[tnr])
            tna = s.op("vector", lambda e: e.tensor_scalar(
                out=srow[0:1, 0:1], in0=srow[0:1, 0:1], scalar1=1e-6,
                scalar2=None, op0=ALU.add), deps=[tns2])
            tnc = s.op("vector", lambda e: e.reciprocal(
                out=srow[0:1, 1:2], in_=srow[0:1, 0:1]), deps=[tna])
            tnb2 = s.op("tensor", lambda e: e.matmul(
                pC[:, 2:3], onesr[0:1, 0:128], srow[0:1, 1:2], start=True,
                stop=True), deps=[tnc])
            tnb3 = s.op("vector", lambda e: e.tensor_copy(
                out=stage[:, 385:386], in_=pC[:, 2:3]), deps=[tnb2])
            tkn = s.op("vector", lambda e, h=h: e.tensor_tensor(
                out=kn_b[:], in0=knt[:, h:h + 1], in1=stage[:, 385:386],
                op=ALU.mult), deps=[tnb3])
            s.barrier()
            if dbg:
                s.dma("gpsimd", dbgwk2_d[:, h * 1024:(h + 1) * 1024], wkh[:])
                s.dma("gpsimd", dbgwv2_d[:, h * 1024:(h + 1) * 1024], wvh[:])
                if h == 7:
                    s.dma("gpsimd", dbgxin7a_d, xin_b[:])
                s.barrier()

            prev_sq2 = None
            for (wsb, dst) in ((wqh, q_fm), (wkh, k_fm)):
                for ti in range(NT):
                    t0 = ti * TT
                    ps, war = srot.acquire()
                    pairs = [(wsb[:, c2 * 128:(c2 + 1) * 128],
                              xin_b[:, c2 * T_FULL + t0:c2 * T_FULL + t0 + TT])
                             for c2 in range(NCH)]
                    tmm = mm_group(s, ps[:, 0:TT], pairs, war_tok=war)
                    ta = s.op("vector", lambda e, ps=ps, t0=t0: e.tensor_tensor(
                        out=rw1[:], in0=ps[0:64, 0:TT], in1=cosb[:, t0:t0 + TT],
                        op=ALU.mult), deps=[tmm] + ([prev_sq2] if prev_sq2 else []))
                    s.op("vector", lambda e, ps=ps, t0=t0: e.tensor_tensor(
                        out=rw2[:], in0=ps[64:128, 0:TT], in1=sinb[:, t0:t0 + TT],
                        op=ALU.mult))
                    s.op("vector", lambda e: e.tensor_tensor(
                        out=rw1[:], in0=rw1[:], in1=rw2[:], op=ALU.subtract))
                    s.op("vector", lambda e, ps=ps, t0=t0: e.tensor_tensor(
                        out=rw2[:], in0=ps[0:64, 0:TT], in1=sinb[:, t0:t0 + TT],
                        op=ALU.mult))
                    tb_ = s.op("vector", lambda e, ps=ps, t0=t0: e.tensor_tensor(
                        out=rw3[:], in0=ps[64:128, 0:TT], in1=cosb[:, t0:t0 + TT],
                        op=ALU.mult))
                    trr = s.op("vector", lambda e: e.tensor_tensor(
                        out=rw2[:], in0=rw2[:], in1=rw3[:], op=ALU.add))
                    srot.release(trr)
                    tq1 = s.op("scalar", lambda e: e.activation(
                        out=sk_b[0:64, 0:TT], in_=rw1[:], func=AF.Square),
                        deps=[trr])
                    tq2 = s.op("scalar", lambda e: e.activation(
                        out=sk_b[0:64, 512:512 + TT], in_=rw2[:], func=AF.Square))
                    prev_sq2 = tq2
                    s.op("tensor", lambda e: e.matmul(
                        pC[0:1, 0:TT], ones1_b[0:64, 0:1], sk_b[0:64, 0:TT],
                        start=True, stop=False), deps=[tq1])
                    tnm = s.op("tensor", lambda e: e.matmul(
                        pC[0:1, 0:TT], ones1_b[0:64, 0:1],
                        sk_b[0:64, 512:512 + TT],
                        start=False, stop=True), deps=[tq2])
                    tsr = s.op("scalar", lambda e: e.activation(
                        out=srow[0:1, 0:TT], in_=pC[0:1, 0:TT], func=AF.Sqrt,
                        bias=0.0, scale=1.0), deps=[tnm])
                    ts2 = s.op("vector", lambda e: e.tensor_scalar(
                        out=srow[0:1, 0:TT], in0=srow[0:1, 0:TT], scalar1=1e-6,
                        scalar2=None, op0=ALU.add), deps=[tsr])
                    ts3 = s.op("vector", lambda e: e.reciprocal(
                        out=srow[0:1, 0:TT], in_=srow[0:1, 0:TT]))
                    tbb = s.op("tensor", lambda e: e.matmul(
                        pC[:, 0:TT], onesr[0:1, 0:128], srow[0:1, 0:TT],
                        start=True, stop=True), deps=[ts3])
                    s.op("vector", lambda e, dst=dst, t0=t0: e.tensor_tensor(
                        out=dst[0:64, t0:t0 + TT], in0=rw1[:], in1=pC[0:64, 0:TT],
                        op=ALU.mult), deps=[tbb])
                    tdn = s.op("vector", lambda e, dst=dst, t0=t0: e.tensor_tensor(
                        out=dst[64:128, t0:t0 + TT], in0=rw2[:],
                        in1=pC[64:128, 0:TT], op=ALU.mult))
                    s.barrier()
            s.barrier()
            if dbg:
                s.dma("gpsimd", dbgkfull_d[:, h * T_FULL:(h + 1) * T_FULL],
                      k_fm[:])
                if h == 7:
                    s.dma("gpsimd", dbgxin7b_d, xin_b[:])
                s.barrier()
            for ti in range(32):
                ps, war = srot.acquire()
                pairs = [(xin_b[:, c2 * T_FULL + ti * 128:c2 * T_FULL + (ti + 1) * 128],
                          wvh[:, c2 * 128:(c2 + 1) * 128])
                         for c2 in range(NCH)]
                tmm = mm_group(s, ps[:, 0:128], pairs, war_tok=war)
                tsc = s.op("vector", lambda e, ps=ps, ti=ti: e.tensor_scalar(
                    out=zb_f[:, 0:128], in0=ps[:, 0:128],
                    scalar1=rvp_cols[:, ti:ti + 1], scalar2=None, op0=ALU.mult),
                    deps=[tmm])
                srot.release(tsc)
                tvb = s.op("vector", lambda e, ti=ti, h=h: e.tensor_tensor(
                    out=v_b[:, ti * 128:(ti + 1) * 128], in0=zb_f[:, 0:128],
                    in1=biasvb[:, h * 128:(h + 1) * 128], op=ALU.add))
                if ti % 4 == 3:
                    s.barrier()
            s.barrier()
            if dbg and h == 0:
                s.dma("gpsimd", dbgv2_d, v_b[:])
                s.dma("gpsimd", dbgrv_d, rvp_cols[:])
                s.dma("gpsimd", dbgbv_d, biasvb[:])
                s.barrier()
            if dbg:
                s.dma("gpsimd", dbgvfull_d[:, h * T_FULL:(h + 1) * T_FULL],
                      v_b[:])
                s.barrier()
            for ti in range(NT):
                t0 = ti * TT
                src = xin_b[:, h * T_FULL + t0:h * T_FULL + t0 + TT]
                tq1 = s.op("scalar", lambda e, src=src: e.activation(
                    out=sk_b[:, 0:TT], in_=src, func=AF.Square))
                tnm = s.op("tensor", lambda e: e.matmul(
                    pC[0:1, 0:TT], ones1_b[:, 0:1], sk_b[:, 0:TT],
                    start=True, stop=True), deps=[tq1])
                tsr = s.op("scalar", lambda e: e.activation(
                    out=srow[0:1, 0:TT], in_=pC[0:1, 0:TT], func=AF.Sqrt,
                    bias=0.0, scale=1.0), deps=[tnm])
                ts2 = s.op("vector", lambda e: e.tensor_scalar(
                    out=srow[0:1, 0:TT], in0=srow[0:1, 0:TT],
                    scalar1=epsr[0:1, 2:3], scalar2=None, op0=ALU.max), deps=[tsr])
                ts3 = s.op("vector", lambda e: e.reciprocal(
                    out=srow[0:1, 0:TT], in_=srow[0:1, 0:TT]))
                tbb = s.op("tensor", lambda e: e.matmul(
                    pC[:, 0:TT], onesr[0:1, 0:128], srow[0:1, 0:TT],
                    start=True, stop=True), deps=[ts3])
                txv = s.op("vector", lambda e, src=src, t0=t0: e.tensor_tensor(
                    out=xv_b[:, t0:t0 + TT], in0=src, in1=pC[:, 0:TT],
                    op=ALU.mult), deps=[tbb])
                s.barrier()
            s.barrier()
            for ti in range(NT):
                t0 = ti * TT
                ps, war = srot.acquire()
                tmm = s.op("tensor", lambda e, ps=ps, t0=t0: e.matmul(
                    ps[:, 0:TT], sh_b[:], xv_b[:, t0:t0 + TT],
                    start=True, stop=True), deps=[war] if war else [])
                tev = s.op("vector", lambda e, ps=ps, t0=t0: e.tensor_copy(
                    out=un_b[:, t0:t0 + TT], in_=ps[:, 0:TT]), deps=[tmm])
                srot.release(tev)
            s.barrier()
            prev_cp = None
            for (src, dstp) in ((q_fm, qf_b), (k_fm, kf_b)):
                for ti in range(32):
                    ttr = s.op("tensor", lambda e, src=src, ti=ti: e.transpose(
                        pT[:, 0:128], src[:, ti * 128:(ti + 1) * 128], ident[:]),
                        deps=[prev_cp] if prev_cp else [])
                    tcp = s.op("vector", lambda e: e.tensor_copy(
                        out=trs_b[:], in_=pT[:, 0:128]), deps=[ttr])
                    prev_cp = tcp
                    s.op("tensor", lambda e, ti=ti: e.matmul(
                        pC[:, 0:128], etm[:, ti * 128:(ti + 1) * 128], trs_b[:],
                        start=(ti == 0), stop=(ti == 31)), deps=[tcp])
                tqf = s.op("vector", lambda e, dstp=dstp: e.tensor_copy(
                    out=dstp[:], in_=pC[:, 0:128]),
                    deps=[("pt", s.cnt["pt"])])
                prev_cp = tqf
                s.barrier()
            for ti in range(32):
                s.op("tensor", lambda e, ti=ti: e.matmul(
                    pC[:, 128:256], etm[:, ti * 128:(ti + 1) * 128],
                    v_b[:, ti * 128:(ti + 1) * 128],
                    start=(ti == 0), stop=(ti == 31)),
                    deps=[prev_cp] if ti == 0 else [])
            tvf = ("pt", s.cnt["pt"])
            tvc = s.op("vector", lambda e: e.tensor_copy(
                out=zb_f[:, 0:128], in_=pC[:, 128:256]), deps=[tvf])
            tvs = s.op("scalar", lambda e: e.activation(
                out=zb_f[:, 128:256], in_=zb_f[:, 0:128], func=AF.Square),
                deps=[tvc])
            tvr = s.op("vector", lambda e: e.reduce_sum(
                out=zb_f[:, 256:257], in_=zb_f[:, 128:256], axis=AX.X),
                deps=[tvs])
            tvq = s.op("scalar", lambda e: e.activation(
                out=zb_f[:, 256:257], in_=zb_f[:, 256:257], func=AF.Sqrt,
                bias=0.0, scale=1.0), deps=[tvr])
            tvm = s.op("vector", lambda e: e.tensor_scalar(
                out=zb_f[:, 256:257], in0=zb_f[:, 256:257],
                scalar1=epsr[0:1, 2:3] if False else 1e-12, scalar2=None,
                op0=ALU.max), deps=[tvq])
            tvi = s.op("vector", lambda e: e.reciprocal(
                out=zb_f[:, 256:257], in_=zb_f[:, 256:257]))
            tvn = s.op("vector", lambda e: e.tensor_scalar(
                out=vn_b[:], in0=zb_f[:, 0:128], scalar1=zb_f[:, 256:257],
                scalar2=None, op0=ALU.mult), deps=[tvi])
            s.barrier()
            prev_cp2 = None
            for (srcb, dstb) in ((qf_b, qfT_b), (kf_b, kfT_b), (vn_b, vnT_b)):
                ttr = s.op("tensor", lambda e, srcb=srcb: e.transpose(
                    pT[:, 0:128], srcb[:], ident[:]),
                    deps=[tvn] + ([prev_cp2] if prev_cp2 else []))
                tcp = s.op("vector", lambda e, dstb=dstb: e.tensor_copy(
                    out=dstb[:], in_=pT[:, 0:128]), deps=[ttr])
                prev_cp2 = tcp
            tuf0 = s.op("tensor", lambda e: e.matmul(
                pC[:, 256:384], sh_b[:], vnT_b[:], start=True, stop=True),
                deps=[prev_cp2])
            tuf = s.op("vector", lambda e: e.tensor_copy(
                out=ufT_b[:], in_=pC[:, 256:384]), deps=[tuf0])
            tfc1 = s.op("tensor", lambda e: e.matmul(
                pC[:, 384:512], qfT_b[:], kfT_b[:], start=True, stop=False),
                deps=[tuf])
            tfc2 = s.op("tensor", lambda e: e.matmul(
                pC[:, 384:512], ufT_b[:], vnT_b[:], start=False, stop=True))
            tfc = s.op("vector", lambda e: e.tensor_copy(
                out=fc_b[:], in_=pC[:, 384:512]), deps=[tfc2])
            s.barrier()
            if dbg:
                o0 = h * 2688
                s.dma("gpsimd", dbgall_d[:, o0:o0 + 512], q_fm[:, 0:512])
                s.dma("gpsimd", dbgall_d[:, o0 + 512:o0 + 1024], k_fm[:, 0:512])
                s.dma("gpsimd", dbgall_d[:, o0 + 1024:o0 + 1536], v_b[:, 0:512])
                s.dma("gpsimd", dbgall_d[:, o0 + 1536:o0 + 2048], un_b[:, 0:512])
                s.dma("gpsimd", dbgall_d[:, o0 + 2048:o0 + 2176], fc_b[:])
                s.dma("gpsimd", dbgall_d[:, o0 + 2176:o0 + 2304], qf_b[:])
                s.dma("gpsimd", dbgall_d[:, o0 + 2304:o0 + 2432], kf_b[:])
                s.dma("gpsimd", dbgall_d[:, o0 + 2432:o0 + 2560], vn_b[:])
                s.dma("gpsimd", dbgall_d[:, o0 + 2560:o0 + 2688], xv_b[:, 0:128])
                s.barrier()
            if dbg and h == 0:
                s.dma("gpsimd", dbgqf_d, qf_b[:])
                s.dma("gpsimd", dbgkf_d, kf_b[:])
                s.dma("gpsimd", dbgvn_d, vn_b[:])
                s.dma("gpsimd", dbguf_d, ufT_b[:])
                s.dma("gpsimd", dbgq_d, q_fm[:])
                s.dma("gpsimd", dbgk_d, k_fm[:])
                s.dma("gpsimd", dbgv_d, v_b[:])
                s.dma("gpsimd", dbgu_d, un_b[:])
                s.dma("gpsimd", dbgxv_d, xv_b[:])
                s.dma("gpsimd", dbgfc_d, fc_b[:])
                s.barrier()

            for qg in range(NT):
                qg0 = qg * TT
                nkt = (qg0 + TT) // 128
                tg1m = s.op("tensor", lambda e, qg0=qg0: e.matmul(
                    pC[:, 0:512], fc_b[:], efm[:, qg0:qg0 + TT],
                    start=True, stop=True), deps=[tfc])
                tg1 = s.op("vector", lambda e: e.tensor_copy(
                    out=g1_b[:], in_=pC[:, 0:512]), deps=[tg1m])
                s.wait("tensor", tg1)
                sum_started = False
                prev_ev = [None, None]
                prev_sq = [None, None]
                tsum = tssq = None
                for kt in range(nkt):
                    o = kt * 128 - qg0
                    widx = (o + 128) // 128 if o >= -128 else -1
                    buf = kt % 2
                    ps, war = srot.acquire()
                    tmm = s.op("tensor", lambda e, ps=ps, kt=kt: e.matmul(
                        ps[:, 0:TT], efm[:, kt * 128:(kt + 1) * 128], g1_b[:],
                        start=True, stop=True),
                        deps=([war] if war else []) + [tg1])
                    evdeps = [tmm]
                    if prev_ev[buf]:
                        evdeps.append(prev_ev[buf])
                    if prev_sq[buf]:
                        evdeps.append(prev_sq[buf])
                    if o >= -128:
                        tev = s.op("vector", lambda e, ps=ps, buf=buf, widx=widx:
                                   e.tensor_tensor(
                            out=sk_b[:, buf * 512:buf * 512 + TT],
                            in0=ps[:, 0:TT],
                            in1=wmask[:, widx * 512:widx * 512 + TT],
                            op=ALU.mult), deps=evdeps)
                    else:
                        tev = s.op("vector", lambda e, ps=ps, buf=buf:
                                   e.tensor_copy(
                            out=sk_b[:, buf * 512:buf * 512 + TT],
                            in_=ps[:, 0:TT]), deps=evdeps)
                    srot.release(tev)
                    tsq = s.op("gpsimd", lambda e, buf=buf: e.tensor_tensor(
                        out=sksq_b[:, buf * 512:buf * 512 + TT],
                        in0=sk_b[:, buf * 512:buf * 512 + TT],
                        in1=sk_b[:, buf * 512:buf * 512 + TT], op=ALU.mult),
                        deps=[tev] + ([prev_sq[buf]] if prev_sq[buf] else []))
                    tsum = s.op("tensor", lambda e, buf=buf, kt=kt, nkt=nkt:
                                e.matmul(
                        pA[0:1, 0:TT], ones1_b[:, 0:1],
                        sk_b[:, buf * 512:buf * 512 + TT],
                        start=(kt == 0), stop=False), deps=[tev])
                    prev_ev[buf] = tsum
                    tssq = s.op("tensor", lambda e, buf=buf, kt=kt: e.matmul(
                        pB[0:1, 0:TT], ones1_b[:, 0:1],
                        sksq_b[:, buf * 512:buf * 512 + TT],
                        start=(kt == 0), stop=False), deps=[tsq])
                    prev_sq[buf] = tssq
                s.barrier()
                tnr_ev = prev_ev[0] or prev_ev[1]
                if prev_sq[0]:
                    s.wait("vector", prev_sq[0])
                    s.wait("gpsimd", prev_sq[0])
                if prev_sq[1]:
                    s.wait("vector", prev_sq[1])
                    s.wait("gpsimd", prev_sq[1])
                for sti in range(4):
                    qs0 = qg0 + sti * 128
                    for wi, ktn in enumerate((qs0 // 128 - 1, qs0 // 128)):
                        if ktn < 0:
                            continue
                        ps, war = srot.acquire()
                        t1 = s.op("tensor", lambda e, ps=ps, ktn=ktn, qs0=qs0:
                                  e.matmul(
                            ps[:, 0:128], k_fm[:, ktn * 128:(ktn + 1) * 128],
                            q_fm[:, qs0:qs0 + 128], start=True, stop=False),
                            deps=[war] if war else [])
                        t2 = s.op("tensor", lambda e, ps=ps, ktn=ktn, qs0=qs0:
                                  e.matmul(
                            ps[:, 0:128], xv_b[:, ktn * 128:(ktn + 1) * 128],
                            un_b[:, qs0:qs0 + 128], start=False, stop=True))
                        tev = s.op("vector", lambda e, ps=ps, wi=wi: e.tensor_tensor(
                            out=sk_b[:, 0:128], in0=ps[:, 0:128],
                            in1=nmask[:, wi * 128:(wi + 1) * 128], op=ALU.mult),
                            deps=[t2] + ([tnr_ev] if tnr_ev else []))
                        srot.release(tev)
                        tsq = s.op("gpsimd", lambda e: e.tensor_tensor(
                            out=sksq_b[:, 0:128], in0=sk_b[:, 0:128],
                            in1=sk_b[:, 0:128], op=ALU.mult), deps=[tev])
                        tsum = s.op("tensor", lambda e, sti=sti: e.matmul(
                            pA[0:1, sti * 128:(sti + 1) * 128], ones1_b[:, 0:1],
                            sk_b[:, 0:128], start=False, stop=False), deps=[tev])
                        tnr_ev = tsum
                        tssq = s.op("tensor", lambda e, sti=sti: e.matmul(
                            pB[0:1, sti * 128:(sti + 1) * 128], ones1_b[:, 0:1],
                            sksq_b[:, 0:128], start=False, stop=False),
                            deps=[tsq])
                        tnr_ev = tssq
                tnl = s.op("tensor", lambda e, qg0=qg0: e.matmul(
                    pC[0:1, 0:TT], kn_b[:], q_fm[:, qg0:qg0 + TT],
                    start=True, stop=True), deps=[tg1])
                tnc = s.op("vector", lambda e: e.tensor_copy(
                    out=snull_r[0:1, 0:TT], in_=pC[0:1, 0:TT]), deps=[tnl])
                tnsq = s.op("scalar", lambda e: e.activation(
                    out=snull_r[0:1, 512:512 + TT], in_=snull_r[0:1, 0:TT],
                    func=AF.Square), deps=[tnc])
                tps = s.op("tensor", lambda e: e.matmul(
                    pA[0:1, 0:TT], onesr[0:1, 0:1], snull_r[0:1, 0:TT],
                    start=False, stop=True), deps=[tnc, tsum, tnr_ev])
                tps2 = s.op("tensor", lambda e: e.matmul(
                    pB[0:1, 0:TT], onesr[0:1, 0:1],
                    snull_r[0:1, 512:512 + TT], start=False, stop=True),
                    deps=[tnsq, tssq])
                rmu = srow[0:1, 0:512]
                rsd = srow[0:1, 512:1024]
                rnn = snull_r[0:1, 512:1024]
                tnl0 = s.dma("sync", nrowc[0:1, 0:TT], nrow_d[0:1, qg0:qg0 + TT],
                             deps=[tps])
                tn1 = s.op("vector", lambda e: e.reciprocal(
                    out=rnn, in_=nrowc[0:1, 0:TT]), deps=[tps, tps2, tnl0])
                tn2 = s.op("vector", lambda e: e.tensor_tensor(
                    out=rmu, in0=pA[0:1, 0:TT], in1=rnn, op=ALU.mult))
                tn3 = s.op("vector", lambda e: e.tensor_tensor(
                    out=rsd, in0=pB[0:1, 0:TT], in1=rnn, op=ALU.mult))
                tn4 = s.op("vector", lambda e: e.tensor_tensor(
                    out=zb_f[0:1, 0:512], in0=rmu, in1=rmu, op=ALU.mult))
                tn5 = s.op("vector", lambda e: e.tensor_tensor(
                    out=rsd, in0=rsd, in1=zb_f[0:1, 0:512], op=ALU.subtract))
                tn6 = s.op("vector", lambda e: e.tensor_scalar(
                    out=rsd, in0=rsd, scalar1=epsr[0:1, 1:2], scalar2=None,
                    op0=ALU.max))
                tn7 = s.op("scalar", lambda e: e.activation(
                    out=rsd, in_=rsd, func=AF.Sqrt, bias=0.0, scale=1.0),
                    deps=[tn6])
                tn8 = s.op("vector", lambda e: e.reciprocal(out=rsd, in_=rsd),
                           deps=[tn7])
                tb1 = s.op("tensor", lambda e: e.matmul(
                    pS0[:, 0:TT], onesr[0:1, 0:128], rmu, start=True, stop=True),
                    deps=[tn8] + [t for t in srot.toks if t] +
                         [t for t in (prev_ev[0], prev_ev[1]) if t])
                tb2 = s.op("tensor", lambda e: e.matmul(
                    pS1[:, 0:TT], onesr[0:1, 0:128], rsd, start=True, stop=True))
                tmu = s.op("vector", lambda e: e.tensor_copy(
                    out=mub_sb[:], in_=pS0[:, 0:TT]), deps=[tb1])
                trb = s.op("vector", lambda e: e.tensor_copy(
                    out=rbb_sb[:], in_=pS1[:, 0:TT]), deps=[tb2])
                s.wait("tensor", tmu)
                s.wait("tensor", trb)
                s.barrier()
                srot.toks = [None, None]
                prev_p = [None, None]
                prev_tex = None
                tzlast = typv = None
                for kt in range(nkt):
                    o = kt * 128 - qg0
                    widx = (o + 128) // 128 if o >= -128 else -1
                    buf = kt % 2
                    ps, war = srot.acquire()
                    tmm = s.op("tensor", lambda e, ps=ps, kt=kt: e.matmul(
                        ps[:, 0:TT], efm[:, kt * 128:(kt + 1) * 128], g1_b[:],
                        start=True, stop=True), deps=[war] if war else [])
                    t1 = s.op("vector", lambda e, ps=ps: e.tensor_tensor(
                        out=zb_f[:], in0=ps[:, 0:TT], in1=mub_sb[:],
                        op=ALU.subtract),
                        deps=[tmm] + ([prev_tex] if prev_tex else []))
                    srot.release(t1)
                    t2 = s.op("vector", lambda e: e.tensor_tensor(
                        out=zb_f[:], in0=zb_f[:], in1=rbb_sb[:], op=ALU.mult))
                    if widx >= 0:
                        t2 = s.op("vector", lambda e, widx=widx: e.tensor_tensor(
                            out=zb_f[:], in0=zb_f[:],
                            in1=wbias[:, widx * 512:widx * 512 + TT],
                            op=ALU.add))
                    tex = s.op("scalar", lambda e, buf=buf: e.activation(
                        out=p_bf[:, buf * 512:buf * 512 + TT], in_=zb_f[:],
                        func=AF.Exp),
                        deps=[t2] + ([prev_p[buf]] if prev_p[buf] else []))
                    prev_tex = tex
                    tz = s.op("tensor", lambda e, buf=buf, kt=kt, nkt=nkt: e.matmul(
                        pZr[0:1, 0:TT], ones1_b[:, 0:1],
                        p_bf[:, buf * 512:buf * 512 + TT],
                        start=(kt == 0), stop=False), deps=[tex])
                    tpv = s.op("tensor", lambda e, buf=buf, kt=kt: e.matmul(
                        pY[:, 0:TT], v_b[:, kt * 128:(kt + 1) * 128],
                        p_bf[:, buf * 512:buf * 512 + TT],
                        start=(kt == 0), stop=False))
                    prev_p[buf] = tpv
                    tzlast, typv = tz, tpv
                s.barrier()
                for sti in range(4):
                    qs0 = qg0 + sti * 128
                    for wi, ktn in enumerate((qs0 // 128 - 1, qs0 // 128)):
                        if ktn < 0:
                            continue
                        ps, war = srot.acquire()
                        t1 = s.op("tensor", lambda e, ps=ps, ktn=ktn, qs0=qs0:
                                  e.matmul(
                            ps[:, 0:128], k_fm[:, ktn * 128:(ktn + 1) * 128],
                            q_fm[:, qs0:qs0 + 128], start=True, stop=False),
                            deps=[war] if war else [])
                        t2 = s.op("tensor", lambda e, ps=ps, ktn=ktn, qs0=qs0:
                                  e.matmul(
                            ps[:, 0:128], xv_b[:, ktn * 128:(ktn + 1) * 128],
                            un_b[:, qs0:qs0 + 128], start=False, stop=True))
                        t3 = s.op("vector", lambda e, ps=ps, sti=sti:
                                  e.tensor_tensor(
                            out=zb_f[:, 0:128], in0=ps[:, 0:128],
                            in1=mub_sb[:, sti * 128:(sti + 1) * 128],
                            op=ALU.subtract),
                            deps=[t2] + ([prev_tex] if prev_tex else []))
                        srot.release(t3)
                        t4 = s.op("vector", lambda e, sti=sti: e.tensor_tensor(
                            out=zb_f[:, 0:128], in0=zb_f[:, 0:128],
                            in1=rbb_sb[:, sti * 128:(sti + 1) * 128],
                            op=ALU.mult))
                        t5 = s.op("vector", lambda e, wi=wi: e.tensor_tensor(
                            out=zb_f[:, 0:128], in0=zb_f[:, 0:128],
                            in1=nbias[:, wi * 128:(wi + 1) * 128], op=ALU.add))
                        tex = s.op("scalar", lambda e: e.activation(
                            out=p_bf[:, 0:128], in_=zb_f[:, 0:128], func=AF.Exp),
                            deps=[t5] + ([prev_p[0]] if prev_p[0] else []))
                        prev_tex = tex
                        tz = s.op("tensor", lambda e, sti=sti: e.matmul(
                            pZr[0:1, sti * 128:(sti + 1) * 128], ones1_b[:, 0:1],
                            p_bf[:, 0:128], start=False, stop=False), deps=[tex])
                        tpv = s.op("tensor", lambda e, ktn=ktn, sti=sti: e.matmul(
                            pY[:, sti * 128:(sti + 1) * 128],
                            v_b[:, ktn * 128:(ktn + 1) * 128],
                            p_bf[:, 0:128], start=False, stop=False))
                        prev_p[0] = tpv
                        tzlast, typv = tz, tpv
                s.barrier()
                tzn1 = s.op("vector", lambda e: e.tensor_tensor(
                    out=snull_r[0:1, 0:TT], in0=snull_r[0:1, 0:TT], in1=rmu,
                    op=ALU.subtract), deps=[tzlast])
                tzn2 = s.op("vector", lambda e: e.tensor_tensor(
                    out=snull_r[0:1, 0:TT], in0=snull_r[0:1, 0:TT], in1=rsd,
                    op=ALU.mult))
                tzn3 = s.op("scalar", lambda e: e.activation(
                    out=snull_r[0:1, 0:TT], in_=snull_r[0:1, 0:TT], func=AF.Exp),
                    deps=[tzn2])
                tzn4 = s.op("vector", lambda e: e.tensor_copy(
                    out=snull_r[0:1, 512:512 + TT], in_=snull_r[0:1, 0:TT]),
                    deps=[tzn3])
                tzn5 = s.op("tensor", lambda e: e.matmul(
                    pZr[0:1, 0:TT], onesr[0:1, 0:1],
                    snull_r[0:1, 512:512 + TT], start=False, stop=True),
                    deps=[tzn4])
                tzc = s.op("vector", lambda e: e.tensor_copy(
                    out=srow[0:1, 0:TT], in_=pZr[0:1, 0:TT]),
                    deps=[tzn5, typv])
                trc = s.op("vector", lambda e: e.reciprocal(
                    out=srow[0:1, 0:TT], in_=srow[0:1, 0:TT]))
                tbz = s.op("tensor", lambda e: e.matmul(
                    pC[:, 0:TT], onesr[0:1, 0:128], srow[0:1, 0:TT],
                    start=True, stop=True), deps=[trc])
                tzb = s.op("vector", lambda e: e.tensor_copy(
                    out=zrb_sb[:], in_=pC[:, 0:TT]), deps=[tbz])
                tye = s.op("vector", lambda e: e.tensor_tensor(
                    out=yev[:], in0=pY[:, 0:TT], in1=zrb_sb[:], op=ALU.mult),
                    deps=[tzb])
                tyd = s.dma("gpsimd", y_scr[h * 128:(h + 1) * 128, qg0:qg0 + TT],
                            yev[:], deps=[tye])
                if dbg:
                    s.dma("gpsimd", dbgy_d[h * 128:(h + 1) * 128, qg0:qg0 + TT],
                          yev[:], deps=[tye])
                if dbg and h == 0 and qg0 == 512:
                    s.dma("gpsimd", dbgg_d, g1_b[:], deps=[tye])
                s.barrier()
        s.barrier()

        bctx.close()
        wo_full = ctx.enter_context(nc.sbuf_tensor("wo_full", [128, NCH * C], BF16))
        sb = lambda name, shape, dt: ctx.enter_context(nc.sbuf_tensor(name, shape, dt))
        ybuf = sb("ybuf", [128, NCH * 512], BF16)
        xres = sb("xres", [128, 512], F32)
        load_cast_weight(s, wo_d, wo_full, stage, C, NCH)
        prev_st = None
        for ti in range(NT):
            t0 = ti * TT
            tly = None
            for c2 in range(NCH):
                tly = s.dma("sync", ybuf[:, c2 * 512:c2 * 512 + TT],
                            y_scr[c2 * 128:(c2 + 1) * 128, t0:t0 + TT],
                            deps=[prev_st] if prev_st and c2 == 0 else [])
            for j in range(NCH):
                ps, war = srot.acquire()
                pairs = [(wo_full[:, c2 * C + j * 128:c2 * C + (j + 1) * 128],
                          ybuf[:, c2 * 512:c2 * 512 + TT]) for c2 in range(NCH)]
                tmm = mm_group(s, ps[:, 0:TT], pairs, war_tok=war,
                               extra_dep=tly)
                tlx = s.dma("sync", xres[:, 0:TT],
                            x_d[j * 128:(j + 1) * 128, t0:t0 + TT],
                            deps=[prev_st] if prev_st else [])
                tfin = s.op("vector", lambda e, ps=ps: e.tensor_tensor(
                    out=xres[:, 0:TT], in0=xres[:, 0:TT], in1=ps[:, 0:TT],
                    op=ALU.add), deps=[tmm, tlx])
                srot.release(tfin)
                prev_st = s.dma("gpsimd",
                                out_d[j * 128:(j + 1) * 128, t0:t0 + TT],
                                xres[:, 0:TT], deps=[tfin])
            s.barrier()
        s.finalize()
    return nc


def _l1_consts():
    T = T_FULL
    half = 64
    inv = 1.0 / 10000.0 ** (np.arange(half, dtype=np.float64) / half)
    ang = np.arange(T, dtype=np.float64)[:, None] * inv[None, :]
    cosb = np.ascontiguousarray(np.cos(ang).T.astype(BF))
    sinb = np.ascontiguousarray(np.sin(ang).T.astype(BF))
    t = np.linspace(-1.0, 1.0, 64)
    W = np.stack([np.ones_like(t), t], axis=1)
    W = W / (np.linalg.norm(W, axis=0, keepdims=True) + 1e-6)
    E = np.zeros((128, T), np.float32)
    for n in range(64):
        E[2 * n, n * 64:(n + 1) * 64] = W[:, 0]
        E[2 * n + 1, n * 64:(n + 1) * 64] = W[:, 1]
    efm = E.astype(BF)
    etm = np.ascontiguousarray(E.T.astype(BF))
    wm = np.zeros((128, 5 * 512), np.float32)
    wb = np.zeros((128, 5 * 512), np.float32)
    kk = np.arange(128)[:, None]
    qq = np.arange(512)[None, :]
    for i, o in enumerate((-128, 0, 128, 256, 384)):
        m = (qq - kk >= o + 65).astype(np.float32)
        wm[:, i * 512:(i + 1) * 512] = m
        wb[:, i * 512:(i + 1) * 512] = -1e9 * (1 - m)
    qq2 = np.arange(128)[None, :]
    nm = np.zeros((128, 256), np.float32)
    nb = np.zeros((128, 256), np.float32)
    for i, o in enumerate((-128, 0)):
        d = qq2 - kk - o
        m = ((d >= 0) & (d <= 64)).astype(np.float32)
        nm[:, i * 128:(i + 1) * 128] = m
        nb[:, i * 128:(i + 1) * 128] = -1e9 * (1 - m)
    nrow = (np.arange(T) + 2).astype(np.float32).reshape(1, T)
    return dict(cosb=cosb, sinb=sinb, efm=efm, etm=etm,
                wmask=wm.astype(BF), wbias=wb.astype(BF),
                nmask=nm.astype(BF), nbias=nb.astype(BF), nrow=nrow)


def run_l1(g, x):
    nc = _get("l1", build_l1)
    cst = _l1_consts()
    base = {
        "wqT": np.ascontiguousarray(g["L1_attn_WQ"].T),
        "wkT": np.ascontiguousarray(g["L1_attn_WK"].T),
        "wvT": np.ascontiguousarray(g["L1_attn_WV"].T),
        "woT": np.ascontiguousarray(g["L1_attn_WO"].T),
        "lnw": g["L1_attn_ln_w"].reshape(C, 1),
        "lnb": g["L1_attn_ln_b"].reshape(C, 1),
        "ah": np.ascontiguousarray(g["L1_attn_A"].reshape(8 * 128, 128)),
        "ahT": np.ascontiguousarray(
            g["L1_attn_A"].transpose(0, 2, 1).reshape(8 * 128, 128)),
        "knullT": np.ascontiguousarray(g["L1_attn_k_null"].reshape(8, 128).T),
        "ident": np.eye(128, dtype=BF),
    }
    base.update(cst)
    in_maps = []
    for c in range(N_CORES):
        m = dict(base)
        m["x"] = _fm(x[c // 2])
        in_maps.append(m)
    res = run_bass_kernel_spmd(nc, in_maps, core_ids=list(range(N_CORES)))
    return np.ascontiguousarray(
        np.stack([res.results[2 * b]["xout"].T for b in range(B)]))



def _fm(a):
    return np.ascontiguousarray(a.T)


def _pos_bcast(T):
    pos = np.arange(1, T + 1, dtype=np.float32)
    ps = (1.0 / (1.0 + 0.1 * np.log(pos))).astype(np.float32)
    return np.ascontiguousarray(np.broadcast_to(ps[None, :], (128, T)))


_prog_cache = {}


def _get(name, builder):
    if name not in _prog_cache:
        _prog_cache[name] = builder()
    return _prog_cache[name]


def run_l0(g, x):
    nc = _get("l0", build_l0)
    base = {
        "wqT": np.ascontiguousarray(g["L0_attn_Wq"].T),
        "wkT": np.ascontiguousarray(g["L0_attn_Wk"].T),
        "wvT": np.ascontiguousarray(g["L0_attn_Wv"].T),
        "wgT": np.ascontiguousarray(g["L0_attn_Wg"].T),
        "woT": np.ascontiguousarray(g["L0_attn_Wo"].T),
        "w0": np.ascontiguousarray(g["L0_attn_W0"]),
        "ainit": np.ascontiguousarray(g["L0_attn_A_init"]),
        "binit": np.ascontiguousarray(g["L0_attn_B_init"]),
        "lng": g["L0_attn_ln_g"].reshape(C, 1),
        "lnb": g["L0_attn_ln_b"].reshape(C, 1),
        "loglr": g["L0_attn_log_lr"].reshape(C, 1),
        "lsg": g["L0_attn_ls_g"].reshape(C, 1),
        "posb": _pos_bcast(T_FULL),
    }
    in_maps = []
    for c in range(N_CORES):
        m = dict(base)
        m["x"] = _fm(x[c // 2])
        in_maps.append(m)
    res = run_bass_kernel_spmd(nc, in_maps, core_ids=list(range(N_CORES)))
    return np.ascontiguousarray(
        np.stack([res.results[2 * b]["xout"].T for b in range(B)]))


def run_l0f(g, x):
    nc = _get("l0f", build_l0f)
    xf = x.reshape(B * T_FULL, C)
    base = {
        "u1": np.ascontiguousarray(g["L0_ffn_U1"]),
        "v1": np.ascontiguousarray(g["L0_ffn_V1"]),
        "b1": g["L0_ffn_b1"].reshape(-1, 1),
        "u2": np.ascontiguousarray(g["L0_ffn_U2"]),
        "v2": np.ascontiguousarray(g["L0_ffn_V2"]),
        "b2": g["L0_ffn_b2"].reshape(-1, 1),
    }
    in_maps = []
    for c in range(N_CORES):
        m = dict(base)
        m["x"] = _fm(xf[c * 2048:(c + 1) * 2048])
        in_maps.append(m)
    res = run_bass_kernel_spmd(nc, in_maps, core_ids=list(range(N_CORES)))
    out = np.concatenate([res.results[c]["xout"].T for c in range(N_CORES)], 0)
    return np.ascontiguousarray(out.reshape(B, T_FULL, C))


def kernel(**inputs):
    g = {k: np.asarray(v) for k, v in inputs.items()}
    x = np.ascontiguousarray(np.asarray(g["L0_attn_x"], np.float32))
    x = run_l0(g, x)
    x = run_l0f(g, x)
    x = run_l1(g, x)
    x = run_l1f(g, x)
    x = run_l2(g, x)
    x = run_l2f(g, x)
    x = run_l3(g, x)
    x = run_l3f(g, x)
    return np.asarray(x, np.float32)
